# version 08
# baseline (speedup 1.0000x reference)
import numpy as np
import ml_dtypes
from contextlib import ExitStack

import concourse.bass as bass
import concourse.mybir as mybir
from concourse.bass_utils import run_bass_kernel_spmd

F32 = mybir.dt.float32
BF16 = mybir.dt.bfloat16
I32 = mybir.dt.int32
AF = mybir.ActivationFunctionType
OP = mybir.AluOpType
BB = ml_dtypes.bfloat16

L = 2048
D = 1024
E = 2048
N = 16
NG = 8
RK = 64
NL = 2
ET = 16
DTL = 8
CH = 2
FC = 1024
H3 = 3072
HPT = 24
FF = 512
NB = 2
EPS = 1e-5

ENGS = ("tensor", "vector", "scalar", "sync", "gpsimd")


class Prog:

    def __init__(self, nc, ctx):
        self.nc = nc
        self.ctx = ctx
        self.ops = {e: [] for e in ENGS}
        self.sem = {e: ctx.enter_context(nc.semaphore("sem_" + e)) for e in ENGS}
        self.cnt = {e: 0 for e in ENGS}
        self.lastwait = {e: {} for e in ENGS}
        self.pending = {e: {} for e in ENGS}

    def chan(self, name):
        if name not in self.sem:
            self.sem[name] = self.ctx.enter_context(
                self.nc.semaphore("ch_" + name))
            self.cnt[name] = 0
        return name

    def want(self, eng, on, val):
        if val <= 0:
            return
        if eng == on and eng in ("tensor", "vector", "scalar"):
            return
        p = self.pending[eng]
        p[on] = max(p.get(on, 0), val)

    def barrier_into(self, engs=ENGS):
        snap = dict(self.cnt)
        for e in engs:
            for o, v in snap.items():
                self.want(e, o, v)

    def run(self, eng, fn, inc=1, inc_key=None):
        pend = [(o, v) for o, v in sorted(self.pending[eng].items())
                if v > self.lastwait[eng].get(o, 0)]
        self.pending[eng] = {}
        for o, v in pend:
            self.lastwait[eng][o] = v
        sem = self.sem
        key = inc_key or eng
        my = self.sem[key]

        def body(e, fn=fn, pend=pend, inc=inc, my=my):
            for o, v in pend:
                e.wait_ge(sem[o], v)
            instr = fn(e)
            if inc:
                instr.then_inc(my, inc)
        self.ops[eng].append(body)
        if inc:
            self.cnt[key] += inc
        return self.cnt[key]

    def finalize(self, blk, final_waits):
        for ename, attr in (("tensor", "tensor"), ("vector", "vector"),
                            ("scalar", "scalar"), ("sync", "sync"),
                            ("gpsimd", "gpsimd")):
            ops = self.ops[ename]
            fw = final_waits.get(ename, [])

            def make(ops=ops, fw=fw):
                def bodyfn(e):
                    for f in ops:
                        f(e)
                    for o, v in fw:
                        e.wait_ge(self.sem[o], v)
                return bodyfn
            getattr(blk, attr)(make())


class Res:

    def __init__(self, p, name=""):
        self.p = p
        self.name = name
        self.prod = None
        self.readers = {}

    def r(self, eng):
        if self.prod:
            self.p.want(eng, self.prod[0], self.prod[1])

    def rdone(self, eng):
        self.readers[eng] = self.p.cnt[eng]

    def w(self, eng):
        if self.prod:
            self.p.want(eng, self.prod[0], self.prod[1])
        for o, v in self.readers.items():
            self.p.want(eng, o, v)

    def wdone(self, eng):
        self.prod = (eng, self.p.cnt[eng])
        self.readers = {}


def build(nc, dump=None, lnfence=True):
    x_h = nc.dram_tensor("x", [NB, D, L], F32, kind="ExternalInput")
    lnw_h = nc.dram_tensor("lnw", [NL, 128, DTL], F32, kind="ExternalInput")
    lnb_h = nc.dram_tensor("lnb", [NL, 128, DTL], F32, kind="ExternalInput")
    inw_h = nc.dram_tensor("inw", [NL, D, 2 * E], F32, kind="ExternalInput")
    convw_h = nc.dram_tensor("convw", [NL, 128, ET, 4], F32, kind="ExternalInput")
    convb_h = nc.dram_tensor("convb", [NL, 128, ET], F32, kind="ExternalInput")
    xpw_h = nc.dram_tensor("xpw", [NL, E, 96], F32, kind="ExternalInput")
    dtpw_h = nc.dram_tensor("dtpw", [NL, RK, E], F32, kind="ExternalInput")
    dtpb_h = nc.dram_tensor("dtpb", [NL, 128, ET], F32, kind="ExternalInput")
    alog_h = nc.dram_tensor("alog", [NL, 128, ET, N], F32, kind="ExternalInput")
    dsk_h = nc.dram_tensor("dsk", [NL, 128, ET], F32, kind="ExternalInput")
    outw_h = nc.dram_tensor("outw", [NL, E, D], F32, kind="ExternalInput")
    nfw_h = nc.dram_tensor("nfw", [128, DTL], F32, kind="ExternalInput")
    nfb_h = nc.dram_tensor("nfb", [128, DTL], F32, kind="ExternalInput")
    wg_h = nc.dram_tensor("wg", [D, H3], I32, kind="ExternalInput")
    wu_h = nc.dram_tensor("wu", [D, H3], I32, kind="ExternalInput")
    wd_h = nc.dram_tensor("wd", [H3, D], I32, kind="ExternalInput")
    sel_h = nc.dram_tensor("sel", [32, 32 * 128], BF16, kind="ExternalInput")
    ones1_h = nc.dram_tensor("ones1", [1, 128], BF16, kind="ExternalInput")
    ones1f_h = nc.dram_tensor("ones1f", [1, 128], F32, kind="ExternalInput")
    onescol_h = nc.dram_tensor("onescol", [128, 1], BF16, kind="ExternalInput")
    y_h = nc.dram_tensor("y", [NB, D, L], F32, kind="ExternalOutput")

    xa_s = nc.dram_tensor("xa_s", [NB, D, L], BF16, kind="Internal")
    x2_s = nc.dram_tensor("x2_s", [NB, D, L], BF16, kind="Internal")
    xc_s = nc.dram_tensor("xc_s", [NB, E, L], BF16, kind="Internal")
    sz_s = nc.dram_tensor("sz_s", [NB, E, L], BF16, kind="Internal")
    hh_s = nc.dram_tensor("hh_s", [NB, H3, L], BF16, kind="Internal")

    dbg = {}
    if dump is not None:
        dbg["xa"] = nc.dram_tensor("dbg_xa", [D, L], BF16, kind="ExternalOutput")
        dbg["lnx"] = nc.dram_tensor("dbg_lnx", [128, DTL * L], BF16,
                                    kind="ExternalOutput")
        dbg["xc"] = nc.dram_tensor("dbg_xc", [E, L], BF16, kind="ExternalOutput")
        dbg["sz"] = nc.dram_tensor("dbg_sz", [E, L], BF16, kind="ExternalOutput")
        dbg["dbl"] = nc.dram_tensor("dbg_dbl", [RK, L], BF16, kind="ExternalOutput")
        dbg["bc"] = nc.dram_tensor("dbg_bc", [32, L], BF16, kind="ExternalOutput")
        dbg["x2"] = nc.dram_tensor("dbg_x2", [D, L], BF16, kind="ExternalOutput")
        dbg["hh"] = nc.dram_tensor("dbg_hh", [H3, L], BF16, kind="ExternalOutput")
        dbg["yg"] = nc.dram_tensor("dbg_yg", [128, ET * FC], BF16,
                                   kind="ExternalOutput")
        dbg["carry"] = nc.dram_tensor("dbg_carry", [128, ET * N], F32,
                                      kind="ExternalOutput")
    if dump == "full":
        dbg["fxa"] = nc.dram_tensor("dbg_fxa", [NB, D, L], BF16,
                                    kind="ExternalOutput")
        dbg["fx2"] = nc.dram_tensor("dbg_fx2", [NB, D, L], BF16,
                                    kind="ExternalOutput")
        dbg["fhh"] = nc.dram_tensor("dbg_fhh", [NB, H3, L], BF16,
                                    kind="ExternalOutput")

    ctx = ExitStack()
    with ctx:
        p = Prog(nc, ctx)
        blk = ctx.enter_context(nc.Block())

        def sb(name, shape, dtype):
            return ctx.enter_context(nc.sbuf_tensor(name, shape, dtype))

        sel_sb = sb("sel_sb", [32, 32 * 128], BF16)
        ones1_sb = sb("ones1_sb", [1, 128], BF16)
        ones1f_sb = sb("ones1f_sb", [1, 128], F32)
        onescol_sb = sb("onescol_sb", [128, 1], BF16)
        murow = sb("murow", [1, L], F32)
        varrow = sb("varrow", [1, L], F32)
        rstdrow = sb("rstdrow", [1, L], F32)
        murep = sb("murep", [128, L], BF16)
        rstdrep = sb("rstdrep", [128, L], F32)
        sp1 = sb("sp1", [128, FC], BF16)
        sp2 = sb("sp2", [128, FC], BF16)

        r_consts = Res(p, "consts")
        r_rows = Res(p, "rows")
        r_rep = Res(p, "rep")
        r_sp = Res(p, "sp")

        ps_a = ctx.enter_context(nc.psum_tensor("ps_a", [128, FC], F32))
        ps_b = ctx.enter_context(nc.psum_tensor("ps_b", [128, FC], F32))
        ps_dbl = ctx.enter_context(nc.psum_tensor("ps_dbl", [96, L], F32))
        r_psa = Res(p, "psa")
        r_psb = Res(p, "psb")
        r_psdbl = Res(p, "psdbl")
        r_xas = Res(p, "xa_s")
        r_x2s = Res(p, "x2_s")
        r_xcs = Res(p, "xc_s")
        r_szs = Res(p, "sz_s")
        r_hhs = Res(p, "hh_s")

        def psum_of(par):
            return (ps_a, r_psa) if (par % 2) == 0 else (ps_b, r_psb)

        def dma(eng_name, out, in_, res_w=None, res_r=None, ch="misc"):
            ch = p.chan(ch)
            p.want(eng_name, ch, p.cnt[ch])
            if res_w:
                for r in res_w:
                    r.w(eng_name)
            if res_r:
                for r in res_r:
                    r.r(eng_name)
            p.run(eng_name, lambda e: e.dma_start(out=out, in_=in_), inc=16,
                  inc_key=ch)
            if res_w:
                for r in res_w:
                    r.wdone(ch)
            if res_r:
                for r in res_r:
                    r.rdone(ch)

        constv = sb("constv", [128, 2], F32)
        sfence_sb = sb("sfence_sb", [1, 8], F32)

        def _cz(e):
            e.memset(constv[:, 0:1], EPS)
            e.memset(sfence_sb[:], 0.0)
            return e.memset(constv[:, 1:2], 1.0)
        p.run("vector", _cz)
        nc.const_aps.register(F32, EPS, constv[:, 0:1])
        nc.const_aps.register(F32, 1.0, constv[:, 1:2])

        dma("sync", sel_sb[:], sel_h.ap(), ch="w")
        dma("sync", ones1_sb[:], ones1_h.ap(), ch="w")
        dma("sync", ones1f_sb[:], ones1f_h.ap(), ch="w")
        dma("sync", onescol_sb[:], onescol_h.ap(), ch="w")
        r_consts.wdone("sync")
        nfw_sb = sb("nfw_sb", [128, DTL], F32)
        nfb_sb = sb("nfb_sb", [128, DTL], F32)
        dma("sync", nfw_sb[:], nfw_h.ap(), ch="w")
        dma("sync", nfb_sb[:], nfb_h.ap(), ch="w")
        r_nf = Res(p, "nf")
        r_nf.wdone("sync")

        def lnfence_mm(e):
            return e.matmul(ps_dbl[64:65, 0:1], onescol_sb[:],
                            onescol_sb[:, 0:1], start=True, stop=True)

        def ln_stats(src, b, xtf, xtb, xsq, r_xtf, r_xtb, r_xsq, r_src=None,
                     src_is_f32=False):
            rs = [r_src] if r_src else None
            for k in range(DTL):
                if src_is_f32:
                    dma("sync", xtf[:], src.ap()[b, k * 128:(k + 1) * 128, :],
                        res_w=[r_xtf], res_r=rs, ch="lx")

                    def cv(e):
                        return e.tensor_copy(xtb[:], xtf[:])
                    r_xtf.r("vector"); r_xtb.w("vector")
                    p.run("vector", cv)
                    r_xtf.rdone("vector"); r_xtb.wdone("vector")
                else:
                    dma("sync", xtb[:], src.ap()[b, k * 128:(k + 1) * 128, :],
                        res_w=[r_xtb], res_r=rs, ch="lx")

                def sq(e):
                    return e.tensor_tensor(xsq[:], xtb[:], xtb[:], OP.mult)
                r_xtb.r("vector"); r_xsq.w("vector")
                p.run("vector", sq)
                r_xtb.rdone("vector"); r_xsq.wdone("vector")
                vc = p.cnt["vector"]

                def mm(e, k=k):
                    last = None
                    for q in range(4):
                        sl = slice(q * 512, (q + 1) * 512)
                        last = e.matmul(ps_dbl[0:1, sl], onescol_sb[:], xtb[:, sl],
                                        start=(k == 0), stop=(k == DTL - 1))
                        last = e.matmul(ps_dbl[32:33, sl], onescol_sb[:], xsq[:, sl],
                                        start=(k == 0), stop=(k == DTL - 1))
                    return last
                p.want("tensor", "vector", vc)
                if k == 0:
                    r_psdbl.w("tensor")
                r_consts.r("tensor")
                if lnfence and k == DTL - 1:
                    p.run("tensor", mm, inc=0)
                    p.run("tensor", lnfence_mm)
                else:
                    p.run("tensor", mm)
                r_xtb.rdone("tensor")
                r_xsq.rdone("tensor")
                r_consts.rdone("tensor")
            r_psdbl.wdone("tensor")

        def ln_rows(rstd_scale=1.0):
            def vrows(e):
                e.tensor_scalar(murow[:], ps_dbl[0:1, :], 1.0 / D, None, OP.mult)
                e.tensor_scalar(varrow[:], ps_dbl[32:33, :], 1.0 / D, None, OP.mult)
                e.tensor_tensor(rstdrow[:], murow[:], murow[:], OP.mult)
                return e.tensor_tensor(varrow[:], varrow[:], rstdrow[:], OP.subtract)
            r_psdbl.r("vector")
            r_rows.w("vector")
            p.run("vector", vrows)
            r_psdbl.rdone("vector")
            vc = p.cnt["vector"]

            def srows(e):
                e.activation(varrow[:], varrow[:], AF.Ln, bias=EPS, scale=1.0)
                return e.activation(rstdrow[:], varrow[:], AF.Exp, bias=0.0,
                                    scale=-0.5 * 1.0)
            p.want("scalar", "vector", vc)
            p.run("scalar", srows)
            if rstd_scale != 1.0:
                def vs(e):
                    return e.tensor_scalar(rstdrow[:], rstdrow[:],
                                           float(rstd_scale), None, OP.mult)
                p.want("vector", "scalar", p.cnt["scalar"])
                p.run("vector", vs)
                r_rows.wdone("vector")
            else:
                r_rows.wdone("scalar")

        def ln_replicate():
            r_rep.w("vector")
            for (row, rep) in ((murow, murep), (rstdrow, rstdrep)):
                for hh in range(2):
                    ps, rps = psum_of(hh)

                    def mm(e, row=row, hh=hh, ps=ps):
                        last = None
                        for q in range(2):
                            base = hh * FC + q * 512
                            last = e.matmul(ps[:, q * 512:(q + 1) * 512],
                                            ones1f_sb[:], row[0:1, base:base + 512],
                                            start=True, stop=True)
                        return last
                    r_rows.r("tensor"); r_consts.r("tensor"); rps.w("tensor")
                    if lnfence:
                        p.run("tensor", mm, inc=0)
                        p.run("tensor", lnfence_mm)
                    else:
                        p.run("tensor", mm)
                    r_rows.rdone("tensor"); rps.wdone("tensor")

                    def cp(e, rep=rep, hh=hh, ps=ps):
                        return e.tensor_copy(rep[:, hh * FC:(hh + 1) * FC], ps[:])
                    rps.r("vector")
                    p.run("vector", cp)
                    rps.rdone("vector")
            r_rep.wdone("vector")

        def load_small_weights(l, W):
            dma("gpsimd", W["outw"][:],
                outw_h.ap()[l].rearrange("(j q) d -> q j d", q=128),
                res_w=[W["r"]], ch="w")
            dma("gpsimd", W["dtpw"][:], dtpw_h.ap()[l], ch="w")
            dma("gpsimd", W["xpw"][:],
                xpw_h.ap()[l].rearrange("(j q) r -> q j r", q=128), ch="w")
            dma("gpsimd", W["convw"][:],
                convw_h.ap()[l], ch="w")
            dma("gpsimd", W["convb"][:],
                convb_h.ap()[l], ch="w")
            dma("gpsimd", W["dtpb"][:],
                dtpb_h.ap()[l], ch="w")
            dma("gpsimd", W["dsk"][:],
                dsk_h.ap()[l], ch="w")
            dma("gpsimd", W["atmp"][:],
                alog_h.ap()[l], ch="w")
            dma("gpsimd", W["lnw"][:],
                lnw_h.ap()[l], ch="w")
            dma("gpsimd", W["lnb"][:],
                lnb_h.ap()[l], ch="w")
            gc = p.cnt[p.chan("w")]

            def sexp(e):
                return e.activation(W["A"][:], W["atmp"][:], AF.Exp, bias=0.0,
                                    scale=1.0)
            p.want("scalar", "w", gc)
            p.run("scalar", sexp)

            def vneg(e):
                return e.tensor_scalar(W["A"][:], W["A"][:], -1.0, None, OP.mult)
            p.want("vector", "scalar", p.cnt["scalar"])
            p.run("vector", vneg)
            W["r"].wdone("vector")

        def phase_pre(l, b, W, S):
            xtf, xtb, xsq = S["xtf"], S["xtb"], S["xsq"]
            r_xtf, r_xtb = S["r_xtf"], S["r_xtb"]
            if l == 0:
                ln_stats(x_h, b, xtf, xtb, xsq, r_xtf, r_xtb, S["r_xsq"], src_is_f32=True)
                ln_rows()
                ln_replicate()
                for k in range(DTL):
                    dma("sync", xtf[:], x_h.ap()[b, k * 128:(k + 1) * 128, :],
                        res_w=[r_xtf], ch="lx")

                    def ap1(e, k=k):
                        e.tensor_copy(xtb[:], xtf[:])
                        e.tensor_tensor(xsq[:], xtb[:], murep[:], OP.subtract)
                        e.tensor_tensor(xtb[:], xsq[:], rstdrep[:], OP.mult)
                        return e.tensor_tensor(xsq[:], xtb[:], xtb[:], OP.mult)
                    r_xtf.r("vector"); r_xtb.w("vector"); r_rep.r("vector")
                    S["r_xsq"].w("vector")
                    p.run("vector", ap1)
                    r_xtf.rdone("vector"); r_xtb.wdone("vector"); r_rep.rdone("vector")
                    S["r_xsq"].wdone("vector")
                    vc = p.cnt["vector"]
                    dma("sync", xa_s.ap()[b, k * 128:(k + 1) * 128, :], xtb[:],
                        res_r=[r_xtb], res_w=[r_xas], ch="sxa")

                    def mm(e, k=k):
                        last = None
                        for q in range(4):
                            sl = slice(q * 512, (q + 1) * 512)
                            last = e.matmul(ps_dbl[0:1, sl], onescol_sb[:],
                                            xtb[:, sl], start=(k == 0),
                                            stop=(k == DTL - 1))
                            last = e.matmul(ps_dbl[32:33, sl], onescol_sb[:],
                                            xsq[:, sl], start=(k == 0),
                                            stop=(k == DTL - 1))
                        return last
                    p.want("tensor", "vector", vc)
                    if k == 0:
                        r_psdbl.w("tensor")
                    if lnfence and k == DTL - 1:
                        p.run("tensor", mm, inc=0)
                        p.run("tensor", lnfence_mm)
                    else:
                        p.run("tensor", mm)
                    r_xtb.rdone("tensor")
                    S["r_xsq"].rdone("tensor")
                r_psdbl.wdone("tensor")
            else:
                ln_stats(xa_s, b, xtf, xtb, xsq, r_xtf, r_xtb, S["r_xsq"], r_src=r_xas)
            ln_rows()
            ln_replicate()
            for k in range(DTL):
                dma("sync", xtb[:], xa_s.ap()[b, k * 128:(k + 1) * 128, :],
                    res_w=[r_xtb], res_r=[r_xas], ch="lx")

                def ap2(e, k=k):
                    e.tensor_tensor(xsq[:], xtb[:], murep[:], OP.subtract)
                    return e.tensor_tensor(xsq[:], xsq[:], rstdrep[:], OP.mult)
                r_xtb.r("vector"); r_rep.r("vector"); S["r_xsq"].w("vector")
                p.run("vector", ap2)
                r_xtb.rdone("vector"); r_rep.rdone("vector")
                S["r_xsq"].wdone("vector")
                vc = p.cnt["vector"]

                def ap3(e, k=k):
                    return e.tensor_scalar(S["lnx"][:, k * L:(k + 1) * L], xsq[:],
                                           W["lnw"][:, k:k + 1],
                                           W["lnb"][:, k:k + 1],
                                           OP.mult, OP.add)
                if k == 0:
                    S["r_lnx"].w("vector")
                W["r"].r("vector")
                p.run("vector", ap3)
            S["r_lnx"].wdone("vector")

        def phase_P(l, b, W, S):
            lnx, r_lnx = S["lnx"], S["r_lnx"]
            inwh, r_inw = S["inw"], S["r_inw"]
            xcpre, r_xcpre = S["xcpre"], S["r_xcpre"]
            accb, r_accb = S["accb"], S["r_accb"]
            xcj, r_xcj = S["xcj"], S["r_xcj"]

            for zphase in range(2):
                dma("gpsimd", inwh[:],
                    inw_h.ap()[l, :, zphase * E:(zphase + 1) * E]
                    .rearrange("(k q) e -> q k e", q=128),
                    res_w=[r_inw], ch="w")
                for j in range(ET):
                    par = j % 2
                    ps, rps = psum_of(par)
                    for half in range(2):
                        def mm(e, j=j, half=half, ps=ps):
                            last = None
                            for q in range(2):
                                for k in range(DTL):
                                    last = e.matmul(
                                        ps[:, q * 512:(q + 1) * 512],
                                        inwh[:, k * E + j * 128:k * E + j * 128 + 128],
                                        lnx[:, k * L + half * FC + q * 512:
                                            k * L + half * FC + (q + 1) * 512],
                                        start=(k == 0), stop=(k == DTL - 1))
                            return last
                        r_lnx.r("tensor"); r_inw.r("tensor"); rps.w("tensor")
                        p.run("tensor", mm)
                        r_lnx.rdone("tensor"); r_inw.rdone("tensor")
                        rps.wdone("tensor")
                        tc = p.cnt["tensor"]

                        if zphase == 0:
                            def ev(e, j=j, half=half, par=par, ps=ps):
                                return e.activation(
                                    xcpre[:, 3 + half * FC:3 + (half + 1) * FC],
                                    ps[:, 0:FC], AF.Copy, bias=0.0, scale=1.0)
                            p.want("scalar", "tensor", tc)
                            if half == 0:
                                r_xcpre.w("scalar")
                            p.run("scalar", ev)
                            rps.rdone("scalar")
                            if half == 1:
                                r_xcpre.wdone("scalar")
                        else:
                            def ev(e, j=j, half=half, ps=ps):
                                return e.activation(
                                    accb[:, half * FC:(half + 1) * FC],
                                    ps[:, 0:FC], AF.Silu, bias=0.0, scale=1.0)
                            p.want("scalar", "tensor", tc)
                            if half == 0:
                                r_accb.w("scalar")
                            p.run("scalar", ev)
                            rps.rdone("scalar")
                            if half == 1:
                                r_accb.wdone("scalar")

                    if zphase == 0:
                        def c0(e, j=j):
                            return e.tensor_scalar(accb[:], xcpre[:, 3:3 + L],
                                                   W["convw"][:, j * 4 + 3:j * 4 + 4],
                                                   W["convb"][:, j:j + 1],
                                                   OP.mult, OP.add)
                        r_xcpre.r("vector"); r_accb.w("vector"); W["r"].r("vector")
                        p.run("vector", c0)
                        r_xcpre.rdone("vector")

                        def c1(e, j=j):
                            last = None
                            for kk in range(3):
                                last = e.scalar_tensor_tensor(
                                    accb[:], xcpre[:, kk:kk + L],
                                    W["convw"][:, j * 4 + kk:j * 4 + kk + 1],
                                    accb[:], OP.mult, OP.add)
                            return last
                        r_accb.w("vector")
                        p.run("vector", c1)
                        r_accb.wdone("vector")
                        r_xcpre.rdone("vector")
                        vc = p.cnt["vector"]

                        def c2(e, j=j):
                            return e.activation(xcj[:], accb[:], AF.Silu,
                                                bias=0.0, scale=1.0)
                        p.want("scalar", "vector", vc)
                        r_xcj.w("scalar")
                        p.run("scalar", c2)
                        r_accb.rdone("scalar")
                        r_xcj.wdone("scalar")
                        sc2 = p.cnt["scalar"]

                        def xp(e, j=j):
                            last = None
                            for q in range(4):
                                last = e.matmul(
                                    ps_dbl[:, q * 512:(q + 1) * 512],
                                    W["xpw"][:, j * 96:(j + 1) * 96],
                                    xcj[:, q * 512:(q + 1) * 512],
                                    start=(j == 0), stop=(j == ET - 1))
                            return last
                        p.want("tensor", "scalar", sc2)
                        if j == 0:
                            r_psdbl.w("tensor")
                        W["r"].r("tensor")
                        p.run("tensor", xp)
                        r_xcj.rdone("tensor")
                        if j == ET - 1:
                            r_psdbl.wdone("tensor")
                        dma("sync", xc_s.ap()[b, j * 128:(j + 1) * 128, :], xcj[:],
                            res_r=[r_xcj], res_w=[r_xcs], ch="sxc")
                    else:
                        dma("sync", sz_s.ap()[b, j * 128:(j + 1) * 128, :], accb[:],
                            res_r=[r_accb], res_w=[r_szs], ch="ssz")

                if zphase == 0:
                    def dev(e, b=b):
                        e.activation(W["dbl_dt"][b][:], ps_dbl[0:RK, :], AF.Copy,
                                     bias=0.0, scale=1.0)
                        return e.activation(W["bc"][b][:], ps_dbl[64:96, :],
                                            AF.Copy, bias=0.0, scale=1.0)
                    r_psdbl.r("scalar"); W["r_dbl"][b].w("scalar")
                    p.run("scalar", dev)
                    r_psdbl.rdone("scalar")
                    W["r_dbl"][b].wdone("scalar")

        class _DumpDone(Exception):
            pass

        def phase_S(l, b, W, S, dbgstop=None):
            bcrep, r_bcrep = S["bcrep"], S["r_bcrep"]
            dt_t, r_dt = S["dt_t"], S["r_dt"]
            u_t, r_u = S["u_t"], S["r_u"]
            xc_t, r_xct = S["xc_t"], S["r_xct"]
            sz_t, r_szt = S["sz_t"], S["r_szt"]
            dA_t, r_dA = S["dA_t"], S["r_dA"]
            dBu_t, r_dBu = S["dBu_t"], S["r_dBu"]
            h_t, r_h = S["h_t"], S["r_h"]
            g_t, r_g = S["g_t"], S["r_g"]
            lad, r_lad = S["lad"], S["r_lad"]
            yacc, r_yacc = S["yacc"], S["r_yacc"]
            carry, r_carry = S["carry"], S["r_carry"]
            yg, r_yg = S["yg"], S["r_yg"]
            xares, r_xares = S["xares"], S["r_xares"]
            xaout, r_xaout = S["xaout"], S["r_xaout"]

            for ch in range(CH):
                tc0 = ch * FC
                for grp in range(N // NG):
                    for i in range(2 * NG):
                        is_c = i >= NG
                        n = grp * NG + (i - NG if is_c else i)
                        selrow = (16 + n) if is_c else n
                        par = i % 2
                        ps, rps = psum_of(par)

                        def mm(e, selrow=selrow, ps=ps, b=b, tc0=tc0):
                            last = None
                            for q in range(2):
                                last = e.matmul(
                                    ps[:, q * 512:(q + 1) * 512],
                                    sel_sb[:, selrow * 128:(selrow + 1) * 128],
                                    W["bc"][b][0:32, tc0 + q * 512:tc0 + (q + 1) * 512],
                                    start=True, stop=True)
                            return last
                        W["r_dbl"][b].r("tensor"); r_consts.r("tensor")
                        rps.w("tensor")
                        p.run("tensor", mm)
                        rps.wdone("tensor")
                        tcn = p.cnt["tensor"]

                        nn = i - NG if is_c else i
                        col = (2 * nn + (1 if is_c else 0)) * FC

                        def cp(e, col=col, ps=ps):
                            return e.tensor_copy(bcrep[:, col:col + FC], ps[:, 0:FC])
                        p.want("vector", "tensor", tcn)
                        if i == 0:
                            r_bcrep.w("vector")
                        p.run("vector", cp)
                        rps.rdone("vector")
                    r_bcrep.wdone("vector")

                    for j in range(ET):
                        par = j % 2
                        ps, rps = psum_of(par)
                        dma("sync", xc_t[:],
                            xc_s.ap()[b, j * 128:(j + 1) * 128, tc0:tc0 + FC],
                            res_w=[r_xct], res_r=[r_xcs], ch="lxc")
                        if grp == 1:
                            dma("sync", sz_t[:],
                                sz_s.ap()[b, j * 128:(j + 1) * 128, tc0:tc0 + FC],
                                res_w=[r_szt], res_r=[r_szs], ch="lsz")

                        def mm(e, j=j, ps=ps, b=b, tc0=tc0):
                            last = None
                            for q in range(2):
                                last = e.matmul(
                                    ps[:, q * 512:(q + 1) * 512],
                                    W["dtpw"][:, j * 128:(j + 1) * 128],
                                    W["dbl_dt"][b][0:RK, tc0 + q * 512:tc0 + (q + 1) * 512],
                                    start=True, stop=True)
                            return last
                        W["r_dbl"][b].r("tensor"); W["r"].r("tensor")
                        rps.w("tensor")
                        p.run("tensor", mm)
                        rps.wdone("tensor")
                        tcn = p.cnt["tensor"]

                        def spl(e, j=j, ps=ps):
                            e.activation(sp1[:], ps[:, 0:FC], AF.Abs,
                                         bias=W["dtpb"][:, j:j + 1], scale=1.0)
                            e.activation(sp1[:], sp1[:], AF.Exp, bias=0.0, scale=-1.0)
                            e.activation(sp1[:], sp1[:], AF.Ln, bias=1.0, scale=1.0)
                            return e.activation(sp2[:], ps[:, 0:FC], AF.Relu,
                                                bias=W["dtpb"][:, j:j + 1], scale=1.0)
                        p.want("scalar", "tensor", tcn)
                        r_sp.w("scalar")
                        p.run("scalar", spl)
                        rps.rdone("scalar")
                        r_sp.wdone("scalar")
                        sc = p.cnt["scalar"]

                        def vdt(e, j=j, grp=grp):
                            e.tensor_tensor(dt_t[:], sp1[:], sp2[:], OP.add)
                            last = e.tensor_tensor(u_t[:], dt_t[:], xc_t[:], OP.mult)
                            if grp == 0:
                                last = e.tensor_scalar(yacc[:], xc_t[:],
                                                       W["dsk"][:, j:j + 1], None,
                                                       OP.mult)
                            return last
                        p.want("vector", "scalar", sc)
                        r_dt.w("vector"); r_u.w("vector"); r_yacc.w("vector")
                        r_xct.r("vector"); r_sp.r("vector")
                        p.run("vector", vdt)
                        r_dt.wdone("vector"); r_u.wdone("vector")
                        r_yacc.wdone("vector"); r_xct.rdone("vector")
                        r_sp.rdone("vector")

                        lad_filled = 0
                        for nn in range(NG):
                            n = grp * NG + nn
                            parA = nn % 2

                            def sda(e, j=j, n=n, parA=parA):
                                return e.activation(
                                    dA_t[parA][:], dt_t[:], AF.Exp, bias=0.0,
                                    scale=W["A"][:, j * N + n:j * N + n + 1])
                            r_dt.r("scalar"); r_dA[parA].w("scalar")
                            W["r"].r("scalar")
                            p.run("scalar", sda)
                            r_dt.rdone("scalar")
                            r_dA[parA].wdone("scalar")
                            scn = p.cnt["scalar"]

                            def vscan(e, j=j, n=n, nn=nn, ch=ch, parA=parA):
                                e.tensor_tensor(dBu_t[:], u_t[:],
                                                bcrep[:, (2 * nn) * FC:(2 * nn + 1) * FC],
                                                OP.mult)
                                init = 0.0 if ch == 0 else carry[:, j * N + n:j * N + n + 1]
                                return e.tensor_tensor_scan(h_t[:], dA_t[parA][:],
                                                            dBu_t[:], init,
                                                            OP.mult, OP.add)
                            p.want("vector", "scalar", scn)
                            r_dBu.w("vector"); r_h.w("vector")
                            r_u.r("vector"); r_bcrep.r("vector")
                            if ch == 1:
                                r_carry.r("vector")
                            p.run("vector", vscan)
                            r_dA[parA].rdone("vector")
                            r_dBu.wdone("vector"); r_h.wdone("vector")
                            r_u.rdone("vector"); r_bcrep.rdone("vector")
                            if ch == 1:
                                r_carry.rdone("vector")
                            if ch == 0:
                                vsc = p.cnt["vector"]
                                p.want("vector", "vector", 0)
                                def vcarry(e, j=j, n=n, vsc=vsc):
                                    e.wait_ge(p.sem["vector"], vsc)
                                    return e.tensor_copy(
                                        carry[:, j * N + n:j * N + n + 1],
                                        h_t[:, FC - 1:FC])
                                r_carry.w("vector")
                                p.run("vector", vcarry)
                                r_carry.wdone("vector")

                            def vg(e, nn=nn):
                                return e.tensor_tensor(
                                    g_t[:], h_t[:],
                                    bcrep[:, (2 * nn + 1) * FC:(2 * nn + 2) * FC],
                                    OP.mult)
                            r_g.w("vector")
                            p.run("vector", vg)
                            r_g.wdone("vector")

                            def vmerge(e, nn=nn):
                                k = 0
                                last = None
                                while (nn >> k) & 1:
                                    last = e.tensor_tensor(g_t[:], g_t[:],
                                                           lad[k][:], OP.add)
                                    k += 1
                                if nn == NG - 1:
                                    return last
                                return e.tensor_copy(lad[k][:], g_t[:])
                            r_lad.w("vector"); r_g.w("vector")
                            p.run("vector", vmerge)
                            r_lad.wdone("vector"); r_g.wdone("vector")

                        if grp == 0:
                            def vstash(e, j=j):
                                return e.tensor_tensor(yg[:, j * FC:(j + 1) * FC],
                                                       yacc[:], g_t[:], OP.add)
                            r_yacc.r("vector"); r_g.r("vector")
                            if j == 0:
                                r_yg.w("vector")
                            p.run("vector", vstash)
                            r_yacc.rdone("vector"); r_g.rdone("vector")
                        else:
                            def vfin(e, j=j):
                                e.tensor_tensor(yacc[:], g_t[:],
                                                yg[:, j * FC:(j + 1) * FC], OP.add)
                                return e.tensor_tensor(yg[:, j * FC:(j + 1) * FC],
                                                       yacc[:], sz_t[:], OP.mult)
                            r_g.r("vector"); r_szt.r("vector"); r_yacc.w("vector")
                            p.run("vector", vfin)
                            r_g.rdone("vector"); r_szt.rdone("vector")
                            r_yacc.wdone("vector")
                            if j == ET - 1:
                                r_yg.wdone("vector")

                if dbgstop == f"S{l}c{ch}":
                    dump_sbuf(dbg["yg"], S["yg"])
                    dump_sbuf(dbg["carry"], S["carry"])
                    p.finalize(blk, {"sync": [(k, v) for k, v in p.cnt.items() if v > 0]})
                    raise _DumpDone

                for k in range(DTL):
                    par = k % 2
                    ps, rps = psum_of(par)

                    def mo(e, k=k, ps=ps):
                        last = None
                        for q in range(2):
                            for j in range(ET):
                                last = e.matmul(
                                    ps[:, q * 512:(q + 1) * 512],
                                    W["outw"][:, j * 1024 + k * 128:j * 1024 + k * 128 + 128],
                                    yg[:, j * FC + q * 512:j * FC + (q + 1) * 512],
                                    start=(j == 0), stop=(j == ET - 1))
                        return last
                    r_yg.r("tensor"); W["r"].r("tensor"); rps.w("tensor")
                    p.run("tensor", mo)
                    r_yg.rdone("tensor"); rps.wdone("tensor")
                    tcn = p.cnt["tensor"]

                    dma("sync", xares[:],
                        xa_s.ap()[b, k * 128:(k + 1) * 128, tc0:tc0 + FC],
                        res_w=[r_xares], res_r=[r_xas], ch="lxa")

                    def vres(e, k=k, ps=ps):
                        return e.tensor_tensor(xaout[:], ps[:, 0:FC], xares[:],
                                               OP.add)
                    p.want("vector", "tensor", tcn)
                    r_xares.r("vector"); r_xaout.w("vector")
                    p.run("vector", vres)
                    rps.rdone("vector"); r_xares.rdone("vector")
                    r_xaout.wdone("vector")
                    dma("sync", xa_s.ap()[b, k * 128:(k + 1) * 128, tc0:tc0 + FC],
                        xaout[:], res_r=[r_xaout], res_w=[r_xas], ch="sxa")

        def phase_post(b, S):
            xtf, xtb, xsq = S["xtf"], S["xtb"], S["xsq"]
            r_xtf, r_xtb = S["r_xtf"], S["r_xtb"]
            accb, r_accb = S["accb"], S["r_accb"]
            ln_stats(xa_s, b, xtf, xtb, xsq, r_xtf, r_xtb, S["r_xsq"], r_src=r_xas)
            ln_rows()
            ln_replicate()
            for k in range(DTL):
                dma("sync", xtb[:], xa_s.ap()[b, k * 128:(k + 1) * 128, :],
                    res_w=[r_xtb], res_r=[r_xas], ch="lx")
                dma("sync", xtf[:], x_h.ap()[b, k * 128:(k + 1) * 128, :],
                    res_w=[r_xtf], ch="lx0")

                def ap1(e, k=k):
                    e.tensor_tensor(xsq[:], xtb[:], murep[:], OP.subtract)
                    return e.tensor_tensor(xsq[:], xsq[:], rstdrep[:], OP.mult)
                r_xtb.r("vector"); r_rep.r("vector"); S["r_xsq"].w("vector")
                p.run("vector", ap1)
                r_xtb.rdone("vector"); r_rep.rdone("vector")
                S["r_xsq"].wdone("vector")
                vc = p.cnt["vector"]

                def ap2(e, k=k):
                    e.tensor_scalar(accb[:], xsq[:],
                                    nfw_sb[:, k:k + 1], nfb_sb[:, k:k + 1],
                                    OP.mult, OP.add)
                    return e.tensor_tensor(accb[:], accb[:], xtf[:], OP.add)
                r_nf.r("vector"); r_xtf.r("vector"); r_accb.w("vector")
                p.run("vector", ap2)
                r_xtf.rdone("vector"); r_accb.wdone("vector")
                dma("sync", x2_s.ap()[b, k * 128:(k + 1) * 128, :], accb[:],
                    res_r=[r_accb], res_w=[r_x2s], ch="sx2")

        def phase_ffn_gateup(b, S, serialize=False, fence=False, sfen=False, exp=None):
            def tfence(e):
                return e.matmul(ps_dbl[0:1, 0:1], onescol_sb[:],
                                onescol_sb[:, 0:1], start=True, stop=True)

            def sfence(e):
                return e.activation(sfence_sb[0:1, 0:8], sfence_sb[0:1, 0:8],
                                    AF.Copy, bias=0.0, scale=1.0)
            """LN(x2)/127 -> xt; gate/up/hh -> hh scratch (per FF chunk)."""
            xt = S["xt"]
            x2t = S["x2t"]
            gs, us = S["gs"], S["us"]
            hh = S["hht"]
            r_xt, r_x2t = S["r_xt"], S["r_x2t"]
            r_gu, r_hh = S["r_gu"], S["r_hh"]
            for cc in range(L // FF):
                c0 = cc * FF
                for k in range(DTL):
                    dma("sync", x2t[k][:],
                        x2_s.ap()[b, k * 128:(k + 1) * 128, c0:c0 + FF],
                        res_w=[r_x2t[k]], res_r=[r_x2s], ch=f"lx2{k % 2}")

                    def apl(e, k=k, c0=c0, exp=exp):
                        if exp == "e_xtnodma":
                            e.tensor_tensor(xt[k][:], murep[:, c0:c0 + FF],
                                            murep[:, c0:c0 + FF], OP.subtract)
                            return e.tensor_tensor(xt[k][:], xt[k][:],
                                                   rstdrep[:, c0:c0 + FF],
                                                   OP.mult)
                        e.tensor_tensor(xt[k][:], x2t[k][:],
                                        murep[:, c0:c0 + FF], OP.subtract)
                        return e.tensor_tensor(xt[k][:], xt[k][:],
                                               rstdrep[:, c0:c0 + FF], OP.mult)
                    r_x2t[k].r("vector"); r_rep.r("vector"); r_xt[k].w("vector")
                    p.run("vector", apl)
                    r_x2t[k].rdone("vector"); r_xt[k].wdone("vector")
                if exp == "e_xtlag":
                    def vfence(e):
                        return e.tensor_copy(sfence_sb[0:1, 0:8],
                                             sfence_sb[0:1, 0:8])
                    p.run("vector", vfence)
                if serialize == "chunk":
                    p.barrier_into()
                for hp in range(HPT):
                    par = hp % 2
                    ps, rps = psum_of(par)

                    def mg(e, hp=hp, ps=ps, exp=exp):
                        last = None
                        for k in range(DTL):
                            mov = murep[:, 0:FF] if exp == "e_noxt" else xt[k][:]
                            last = e.matmul(
                                ps[:, 0:FF],
                                S["wg"][:, k * H3 + hp * 128:k * H3 + hp * 128 + 128],
                                mov, start=(k == 0), stop=(k == DTL - 1))
                        return last
                    for k in range(DTL):
                        r_xt[k].r("tensor")
                    S["r_w"].r("tensor"); rps.w("tensor")
                    if fence:
                        p.run("tensor", mg, inc=0)
                        p.run("tensor", tfence)
                    else:
                        p.run("tensor", mg)
                    for k in range(DTL):
                        r_xt[k].rdone("tensor")
                    rps.wdone("tensor")
                    tc1 = p.cnt["tensor"]

                    def sg(e, ps=ps):
                        return e.activation(gs[:], ps[:, 0:FF], AF.Silu,
                                            bias=0.0, scale=1.0)
                    p.want("scalar", "tensor", tc1)
                    r_gu.w("scalar")
                    if sfen:
                        p.run("scalar", sg, inc=0)
                        p.run("scalar", sfence)
                    else:
                        p.run("scalar", sg)
                    rps.rdone("scalar")

                    def mu_(e, hp=hp, ps=ps, exp=exp):
                        last = None
                        for k in range(DTL):
                            mov = murep[:, 0:FF] if exp == "e_noxt" else xt[k][:]
                            last = e.matmul(
                                ps[:, 0:FF],
                                S["wu"][:, k * H3 + hp * 128:k * H3 + hp * 128 + 128],
                                mov, start=(k == 0), stop=(k == DTL - 1))
                        return last
                    rps.w("tensor")
                    if fence:
                        p.run("tensor", mu_, inc=0)
                        p.run("tensor", tfence)
                    else:
                        p.run("tensor", mu_)
                    rps.wdone("tensor")
                    for k in range(DTL):
                        r_xt[k].rdone("tensor")
                    tc2 = p.cnt["tensor"]

                    def su(e, ps=ps):
                        return e.activation(us[:], ps[:, 0:FF], AF.Copy,
                                            bias=0.0, scale=1.0)
                    p.want("scalar", "tensor", tc2)
                    if sfen:
                        p.run("scalar", su, inc=0)
                        p.run("scalar", sfence)
                    else:
                        p.run("scalar", su)
                    rps.rdone("scalar")
                    r_gu.wdone("scalar")
                    sc1 = p.cnt["scalar"]

                    def vh(e, exp=exp):
                        if exp == "e_nosu":
                            return e.tensor_copy(hh[:], gs[:])
                        return e.scalar_tensor_tensor(hh[:], gs[:], 1.0 / 127.0,
                                                      us[:], OP.mult, OP.mult)
                    p.want("vector", "scalar", sc1)
                    r_gu.r("vector"); r_hh.w("vector")
                    p.run("vector", vh)
                    r_gu.rdone("vector"); r_hh.wdone("vector")
                    dma("sync",
                        hh_s.ap()[b, hp * 128:(hp + 1) * 128, c0:c0 + FF],
                        hh[:], res_r=[r_hh], res_w=[r_hhs], ch="shh")
                    if serialize is True or (serialize == "h4" and hp % 4 == 3):
                        p.barrier_into()

        def phase_ffn_down(b, S):
            hh = S["hhall"]
            r_hh = S["r_hhall"]
            x2t = S["x2t"]
            r_x2t = S["r_x2t"]
            outt = S["outt"]
            r_outt = S["r_outt"]
            for cc in range(L // FF):
                c0 = cc * FF
                for hp in range(HPT):
                    dma("sync", hh[hp][:],
                        hh_s.ap()[b, hp * 128:(hp + 1) * 128, c0:c0 + FF],
                        res_w=[r_hh[hp]], res_r=[r_hhs], ch=f"lhh{hp % 4}")
                for k in range(DTL):
                    par = k % 2
                    ps, rps = psum_of(par)
                    dma("sync", x2t[k][:],
                        x2_s.ap()[b, k * 128:(k + 1) * 128, c0:c0 + FF],
                        res_w=[r_x2t[k]], res_r=[r_x2s], ch="lx2a")

                    def md(e, k=k, ps=ps):
                        last = None
                        for hp in range(HPT):
                            last = e.matmul(
                                ps[:, 0:FF],
                                S["wd"][:, hp * 1024 + k * 128:hp * 1024 + k * 128 + 128],
                                hh[hp][:], start=(hp == 0), stop=(hp == HPT - 1))
                        return last
                    for hp in range(HPT):
                        r_hh[hp].r("tensor")
                    S["r_w"].r("tensor"); rps.w("tensor")
                    p.run("tensor", md)
                    for hp in range(HPT):
                        r_hh[hp].rdone("tensor")
                    rps.wdone("tensor")
                    tcn = p.cnt["tensor"]

                    def vo(e, k=k, ps=ps):
                        return e.tensor_tensor(outt[k % 2][:], ps[:, 0:FF],
                                               x2t[k][:], OP.add)
                    p.want("vector", "tensor", tcn)
                    r_x2t[k].r("vector"); r_outt[k % 2].w("vector")
                    p.run("vector", vo)
                    rps.rdone("vector"); r_x2t[k].rdone("vector")
                    r_outt[k % 2].wdone("vector")
                    dma("sync", y_h.ap()[b, k * 128:(k + 1) * 128, c0:c0 + FF],
                        outt[k % 2][:], res_r=[r_outt[k % 2]], ch=f"sy{k % 2}")

        def dump_dram(dst, srcap):
            p.barrier_into(("sync",))
            ch = p.chan("dbg")
            p.want("sync", ch, p.cnt[ch])
            p.run("sync", lambda e, d=dst, s=srcap: e.dma_start(out=d.ap(), in_=s),
                  inc=16, inc_key=ch)

        def dump_sbuf(dst, t):
            p.barrier_into(("sync",))
            ch = p.chan("dbg")
            p.want("sync", ch, p.cnt[ch])
            p.run("sync", lambda e, d=dst, t=t: e.dma_start(out=d.ap(), in_=t[:]),
                  inc=16, inc_key=ch)

        for l in range(NL):
            with ExitStack() as lctx:
                def lsb(name, shape, dtype):
                    return lctx.enter_context(
                        nc.sbuf_tensor(f"{name}_{l}", shape, dtype))
                W = {
                    "outw": lsb("outw", [128, ET * 1024], BF16),
                    "dtpw": lsb("dtpw", [RK, E], BF16),
                    "xpw": lsb("xpw", [128, ET * 96], BF16),
                    "convw": lsb("convw", [128, ET * 4], F32),
                    "convb": lsb("convb", [128, ET], F32),
                    "dtpb": lsb("dtpb", [128, ET], F32),
                    "dsk": lsb("dsk", [128, ET], F32),
                    "atmp": lsb("atmp", [128, ET * N], F32),
                    "A": lsb("A", [128, ET * N], F32),
                    "lnw": lsb("lnw", [128, DTL], F32),
                    "lnb": lsb("lnb", [128, DTL], F32),
                    "dbl_dt": [lsb(f"dbldt{b}", [RK, L], BF16) for b in range(NB)],
                    "bc": [lsb(f"bc{b}", [32, L], BF16) for b in range(NB)],
                }
                W["r"] = Res(p, f"W{l}")
                W["r_dbl"] = [Res(p, f"dbl{l}{b}") for b in range(NB)]
                p.barrier_into(("gpsimd",))
                load_small_weights(l, W)

                with ExitStack() as pctx:
                    def psb(name, shape, dtype):
                        return pctx.enter_context(
                            nc.sbuf_tensor(f"{name}_{l}", shape, dtype))
                    S = {
                        "inw": psb("inw", [128, DTL * E], BF16),
                        "lnx": psb("lnx", [128, DTL * L], BF16),
                        "xcpre": psb("xcpre", [128, L + 4], BF16),
                        "accb": psb("accb", [128, L], BF16),
                        "xcj": psb("xcj", [128, L], BF16),
                        "xtf": psb("xtf", [128, L], F32),
                        "xtb": psb("xtb", [128, L], BF16),
                        "xsq": psb("xsq", [128, L], BF16),
                    }
                    for nm in ("inw", "lnx", "xcpre", "accb", "xcj", "xtf",
                               "xtb", "xsq"):
                        S["r_" + nm] = Res(p, nm)
                    S["r_xtf"], S["r_xtb"] = S["r_xtf"], S["r_xtb"]
                    p.barrier_into()
                    def z3(e, _xp=S["xcpre"]):
                        return e.memset(_xp[:, 0:3], 0.0)
                    p.run("vector", z3)
                    S["r_xcpre"].wdone("vector")
                    for b in range(NB):
                        phase_pre(l, b, W, S)
                        if dump == f"pre{l}" and b == 0:
                            dump_dram(dbg["xa"], xa_s.ap()[0])
                            dump_sbuf(dbg["lnx"], S["lnx"])
                            p.finalize(blk, {"sync": [(k, v) for k, v in p.cnt.items() if v > 0]})
                            return
                        phase_P(l, b, W, S)
                        if dump == f"P{l}" and b == 0:
                            dump_dram(dbg["xc"], xc_s.ap()[0])
                            dump_dram(dbg["sz"], sz_s.ap()[0])
                            dump_sbuf(dbg["dbl"], W["dbl_dt"][0])
                            dump_sbuf(dbg["bc"], W["bc"][0])
                            p.finalize(blk, {"sync": [(k, v) for k, v in p.cnt.items() if v > 0]})
                            return
                    p.barrier_into()

                with ExitStack() as sctx:
                    def ssb(name, shape, dtype):
                        return sctx.enter_context(
                            nc.sbuf_tensor(f"{name}_{l}", shape, dtype))
                    S = {
                        "bcrep": ssb("bcrep", [128, 2 * NG * FC], BF16),
                        "dt_t": ssb("dt_t", [128, FC], BF16),
                        "u_t": ssb("u_t", [128, FC], BF16),
                        "xc_t": ssb("xc_t", [128, FC], BF16),
                        "sz_t": ssb("sz_t", [128, FC], BF16),
                        "dA_t": [ssb(f"dA{i}", [128, FC], F32) for i in range(2)],
                        "dBu_t": ssb("dBu_t", [128, FC], BF16),
                        "h_t": ssb("h_t", [128, FC], BF16),
                        "g_t": ssb("g_t", [128, FC], BF16),
                        "lad": [ssb(f"lad{i}", [128, FC], BF16) for i in range(3)],
                        "yacc": ssb("yacc", [128, FC], BF16),
                        "carry": ssb("carry", [128, ET * N], F32),
                        "yg": ssb("yg", [128, ET * FC], BF16),
                        "xares": ssb("xares", [128, FC], BF16),
                        "xaout": ssb("xaout", [128, FC], BF16),
                    }
                    for nm in ("bcrep", "dt", "u", "xct", "szt", "dBu", "h",
                               "g", "lad", "yacc", "carry", "yg", "xares",
                               "xaout"):
                        S["r_" + nm] = Res(p, nm)
                    S["r_dA"] = [Res(p, f"dA{i}") for i in range(2)]
                    p.barrier_into()
                    for b in range(NB):
                        try:
                            phase_S(l, b, W, S,
                                    dbgstop=dump if b == 0 else None)
                        except _DumpDone:
                            return
                        if dump == f"S{l}" and b == 0:
                            dump_dram(dbg["xa"], xa_s.ap()[0])
                            dump_sbuf(dbg["yg"], S["yg"])
                            p.finalize(blk, {"sync": [(k, v) for k, v in p.cnt.items() if v > 0]})
                            return
                    p.barrier_into()

        with ExitStack() as qctx:
            def qsb(name, shape, dtype):
                return qctx.enter_context(nc.sbuf_tensor(name, shape, dtype))
            S = {
                "xtf": qsb("q_xtf", [128, L], F32),
                "xtb": qsb("q_xtb", [128, L], BF16),
                "xsq": qsb("q_xsq", [128, L], BF16),
                "accb": qsb("q_accb", [128, L], BF16),
            }
            for nm in ("xtf", "xtb", "xsq", "accb"):
                S["r_" + nm] = Res(p, nm)
            p.barrier_into()
            for b in range(NB):
                phase_post(b, S)
                if dump == "post" and b == 0:
                    dump_dram(dbg["x2"], x2_s.ap()[0])
                    p.finalize(blk, {"sync": [(k, v) for k, v in p.cnt.items() if v > 0]})
                    return
            p.barrier_into()

        with ExitStack() as fctx:
            def fsb(name, shape, dtype):
                return fctx.enter_context(nc.sbuf_tensor(name, shape, dtype))
            S = {
                "wg": fsb("wg_sb", [128, DTL * H3], BF16),
                "wu": fsb("wu_sb", [128, DTL * H3], BF16),
                "xt": [fsb(f"f_xt{k}", [128, FF], BF16) for k in range(DTL)],
                "x2t": [fsb(f"f_x2t{k}", [128, FF], BF16) for k in range(DTL)],
                "gs": fsb("f_gs", [128, FF], BF16),
                "us": fsb("f_us", [128, FF], BF16),
                "hht": fsb("f_hht", [128, FF], BF16),
                "xtb": fsb("f_xtb", [128, L], BF16),
                "xsq": fsb("f_xsq", [128, L], BF16),
            }
            S["r_w"] = Res(p, "fw")
            S["r_xt"] = [Res(p, f"fxt{k}") for k in range(DTL)]
            S["r_x2t"] = [Res(p, f"fx2t{k}") for k in range(DTL)]
            S["r_gu"] = Res(p, "fgu")
            S["r_hh"] = Res(p, "fhh")
            S["r_xtb"] = Res(p, "fxtb")
            S["r_xtf"] = Res(p, "fxtf")
            S["r_xsq"] = Res(p, "fxsq")
            p.barrier_into()
            dma("gpsimd", S["wg"][:],
                wg_h.ap().rearrange("(k q) h -> q k h", q=128), res_w=[S["r_w"]],
                ch="w")
            dma("gpsimd", S["wu"][:],
                wu_h.ap().rearrange("(k q) h -> q k h", q=128), ch="w")
            S["r_w"].wdone("gpsimd")
            for b in range(NB):
                ln_stats(x2_s, b, None, S["xtb"], S["xsq"], None, S["r_xtb"], S["r_xsq"],
                         r_src=r_x2s)
                ln_rows(rstd_scale=1.0 / 127.0)
                ln_replicate()
                phase_ffn_gateup(b, S, serialize=({'serf': True, 'serc': 'chunk', 'serh4': 'h4'}.get(dump, False)), fence=(dump in ('v2',)), sfen=(dump in ('v3',)), exp=dump)
                if dump == "ffnA" and b == 0:
                    dump_dram(dbg["hh"], hh_s.ap()[0])
                    p.finalize(blk, {"sync": [(k, v) for k, v in p.cnt.items() if v > 0]})
                    return
            p.barrier_into()

        with ExitStack() as gctx:
            def gsb(name, shape, dtype):
                return gctx.enter_context(nc.sbuf_tensor(name, shape, dtype))
            S2 = {
                "wd": gsb("wd_sb", [128, HPT * 1024], BF16),
                "hhall": [gsb(f"g_hh{i}", [128, FF], BF16) for i in range(HPT)],
                "x2t": [gsb(f"g_x2t{k}", [128, FF], BF16) for k in range(DTL)],
                "outt": [gsb(f"g_out{i}", [128, FF], F32) for i in range(2)],
            }
            S2["r_w"] = Res(p, "gw")
            S2["r_hhall"] = [Res(p, f"ghh{i}") for i in range(HPT)]
            S2["r_x2t"] = [Res(p, f"gx2t{k}") for k in range(DTL)]
            S2["r_outt"] = [Res(p, f"gout{i}") for i in range(2)]
            p.barrier_into()
            dma("gpsimd", S2["wd"][:],
                wd_h.ap().rearrange("(j q) d -> q j d", q=128),
                res_w=[S2["r_w"]], ch="w")
            for b in range(NB):
                phase_ffn_down(b, S2)

        if dump == "full":
            for (d, s) in (("fxa", xa_s), ("fx2", x2_s), ("fhh", hh_s)):
                p.barrier_into(("sync",))
                ch = p.chan("dbg")
                p.want("sync", ch, p.cnt[ch])
                p.run("sync", lambda e, d=dbg[d], s=s: e.dma_start(out=d.ap(),
                                                                  in_=s.ap()),
                      inc=16, inc_key=ch)

        final = {"sync": [(k, v) for k, v in p.cnt.items() if v > 0]}
        p.finalize(blk, final)
    return nc


_SEL = None


def _consts():
    global _SEL
    if _SEL is None:
        sel = np.zeros((32, 32, 128), np.float32)
        for r in range(32):
            sel[r, r, :] = 1.0
        _SEL = sel.reshape(32, 32 * 128).astype(BB)
    return _SEL


def kernel(L0_attn_x, L0_attn_ln_w, L0_attn_ln_b, L0_attn_in_proj_w,
           L0_attn_conv_w, L0_attn_conv_b, L0_attn_x_proj_w, L0_attn_dt_proj_w,
           L0_attn_dt_proj_b, L0_attn_A_log, L0_attn_D_skip, L0_attn_out_proj_w,
           L0_attn_norm_f_w, L0_attn_norm_f_b, L0_ffn_w_gate, L0_ffn_w_up,
           L0_ffn_w_down):
    x = np.asarray(L0_attn_x, np.float32)
    xT = np.ascontiguousarray(x.transpose(0, 2, 1))
    inw = np.ascontiguousarray(np.asarray(L0_attn_in_proj_w, np.float32)
                               .transpose(0, 2, 1))
    xpw = np.ascontiguousarray(np.asarray(L0_attn_x_proj_w, np.float32)
                               .transpose(0, 2, 1))
    dtpw = np.ascontiguousarray(np.asarray(L0_attn_dt_proj_w, np.float32)
                                .transpose(0, 2, 1))
    outw = np.ascontiguousarray(np.asarray(L0_attn_out_proj_w, np.float32)
                                .transpose(0, 2, 1))
    wg = np.ascontiguousarray(np.asarray(L0_ffn_w_gate, np.int32).T)
    wu = np.ascontiguousarray(np.asarray(L0_ffn_w_up, np.int32).T)
    wd = np.ascontiguousarray(np.asarray(L0_ffn_w_down, np.int32).T)

    sel = _consts()
    ones1 = np.ones((1, 128), np.float32).astype(BB)
    ones1f = np.ones((1, 128), np.float32)
    onescol = np.ones((128, 1), np.float32).astype(BB)

    def pp2(a):
        a = np.asarray(a, np.float32)
        t = a.shape[1] // 128
        return np.ascontiguousarray(a.reshape(NL, t, 128).transpose(0, 2, 1))

    def pp3(a):
        a = np.asarray(a, np.float32)
        t, k = a.shape[1] // 128, a.shape[2]
        return np.ascontiguousarray(a.reshape(NL, t, 128, k).transpose(0, 2, 1, 3))

    def pp1(a):
        a = np.asarray(a, np.float32)
        return np.ascontiguousarray(a.reshape(-1, 128).T)

    common = dict(
        lnw=pp2(L0_attn_ln_w),
        lnb=pp2(L0_attn_ln_b),
        inw=inw,
        convw=pp3(L0_attn_conv_w),
        convb=pp2(L0_attn_conv_b),
        xpw=xpw,
        dtpw=dtpw,
        dtpb=pp2(L0_attn_dt_proj_b),
        alog=pp3(L0_attn_A_log),
        dsk=pp2(L0_attn_D_skip),
        outw=outw,
        nfw=pp1(L0_attn_norm_f_w),
        nfb=pp1(L0_attn_norm_f_b),
        wg=wg, wu=wu, wd=wd,
        sel=sel, ones1=ones1, ones1f=ones1f, onescol=onescol,
    )
    in_maps = []
    for c in range(8):
        m = dict(common)
        m["x"] = np.ascontiguousarray(xT[2 * c:2 * c + 2])
        in_maps.append(m)

    nc = bass.Bass("TRN2", target_bir_lowering=False)
    build(nc)
    res = run_bass_kernel_spmd(nc, in_maps, core_ids=list(range(8)))
    outs = []
    for c in range(8):
        yc = np.asarray(res.results[c]["y"], np.float32)
        outs.append(yc.transpose(0, 2, 1))
    return np.ascontiguousarray(np.concatenate(outs, axis=0))


# version 09
# speedup vs baseline: 1.0168x; 1.0168x over previous
import numpy as np
import ml_dtypes
from contextlib import ExitStack

import concourse.bass as bass
import concourse.mybir as mybir
from concourse.bass_utils import run_bass_kernel_spmd

F32 = mybir.dt.float32
BF16 = mybir.dt.bfloat16
I32 = mybir.dt.int32
AF = mybir.ActivationFunctionType
OP = mybir.AluOpType
BB = ml_dtypes.bfloat16

L = 2048
D = 1024
E = 2048
N = 16
NG = 8
RK = 64
NL = 2
ET = 16
DTL = 8
CH = 2
FC = 1024
H3 = 3072
HPT = 24
FF = 512
NB = 2
EPS = 1e-5

ENGS = ("tensor", "vector", "scalar", "sync", "gpsimd")


class Prog:

    def __init__(self, nc, ctx):
        self.nc = nc
        self.ctx = ctx
        self.ops = {e: [] for e in ENGS}
        self.sem = {e: ctx.enter_context(nc.semaphore("sem_" + e)) for e in ENGS}
        self.cnt = {e: 0 for e in ENGS}
        self.lastwait = {e: {} for e in ENGS}
        self.pending = {e: {} for e in ENGS}

    def chan(self, name):
        if name not in self.sem:
            self.sem[name] = self.ctx.enter_context(
                self.nc.semaphore("ch_" + name))
            self.cnt[name] = 0
        return name

    def want(self, eng, on, val):
        if val <= 0:
            return
        if eng == on and eng in ("tensor", "vector", "scalar"):
            return
        p = self.pending[eng]
        p[on] = max(p.get(on, 0), val)

    def barrier_into(self, engs=ENGS):
        snap = dict(self.cnt)
        for e in engs:
            for o, v in snap.items():
                self.want(e, o, v)

    def run(self, eng, fn, inc=1, inc_key=None):
        pend = [(o, v) for o, v in sorted(self.pending[eng].items())
                if v > self.lastwait[eng].get(o, 0)]
        self.pending[eng] = {}
        for o, v in pend:
            self.lastwait[eng][o] = v
        sem = self.sem
        key = inc_key or eng
        my = self.sem[key]

        def body(e, fn=fn, pend=pend, inc=inc, my=my):
            for o, v in pend:
                e.wait_ge(sem[o], v)
            instr = fn(e)
            if inc:
                instr.then_inc(my, inc)
        self.ops[eng].append(body)
        if inc:
            self.cnt[key] += inc
        return self.cnt[key]

    def finalize(self, blk, final_waits):
        for ename, attr in (("tensor", "tensor"), ("vector", "vector"),
                            ("scalar", "scalar"), ("sync", "sync"),
                            ("gpsimd", "gpsimd")):
            ops = self.ops[ename]
            fw = final_waits.get(ename, [])

            def make(ops=ops, fw=fw):
                def bodyfn(e):
                    for f in ops:
                        f(e)
                    for o, v in fw:
                        e.wait_ge(self.sem[o], v)
                return bodyfn
            getattr(blk, attr)(make())


class Res:

    def __init__(self, p, name=""):
        self.p = p
        self.name = name
        self.prod = None
        self.readers = {}

    def r(self, eng):
        if self.prod:
            self.p.want(eng, self.prod[0], self.prod[1])

    def rdone(self, eng):
        self.readers[eng] = self.p.cnt[eng]

    def w(self, eng):
        if self.prod:
            self.p.want(eng, self.prod[0], self.prod[1])
        for o, v in self.readers.items():
            self.p.want(eng, o, v)

    def wdone(self, eng):
        self.prod = (eng, self.p.cnt[eng])
        self.readers = {}


def build(nc, dump=None, lnfence=True):
    x_h = nc.dram_tensor("x", [NB, D, L], F32, kind="ExternalInput")
    lnw_h = nc.dram_tensor("lnw", [NL, 128, DTL], F32, kind="ExternalInput")
    lnb_h = nc.dram_tensor("lnb", [NL, 128, DTL], F32, kind="ExternalInput")
    inw_h = nc.dram_tensor("inw", [NL, D, 2 * E], F32, kind="ExternalInput")
    convw_h = nc.dram_tensor("convw", [NL, 128, ET, 4], F32, kind="ExternalInput")
    convb_h = nc.dram_tensor("convb", [NL, 128, ET], F32, kind="ExternalInput")
    xpw_h = nc.dram_tensor("xpw", [NL, E, 96], F32, kind="ExternalInput")
    dtpw_h = nc.dram_tensor("dtpw", [NL, RK, E], F32, kind="ExternalInput")
    dtpb_h = nc.dram_tensor("dtpb", [NL, 128, ET], F32, kind="ExternalInput")
    alog_h = nc.dram_tensor("alog", [NL, 128, ET, N], F32, kind="ExternalInput")
    dsk_h = nc.dram_tensor("dsk", [NL, 128, ET], F32, kind="ExternalInput")
    outw_h = nc.dram_tensor("outw", [NL, E, D], F32, kind="ExternalInput")
    nfw_h = nc.dram_tensor("nfw", [128, DTL], F32, kind="ExternalInput")
    nfb_h = nc.dram_tensor("nfb", [128, DTL], F32, kind="ExternalInput")
    wg_h = nc.dram_tensor("wg", [D, H3], I32, kind="ExternalInput")
    wu_h = nc.dram_tensor("wu", [D, H3], I32, kind="ExternalInput")
    wd_h = nc.dram_tensor("wd", [H3, D], I32, kind="ExternalInput")
    sel_h = nc.dram_tensor("sel", [32, 32 * 128], BF16, kind="ExternalInput")
    ones1_h = nc.dram_tensor("ones1", [1, 128], BF16, kind="ExternalInput")
    ones1f_h = nc.dram_tensor("ones1f", [1, 128], F32, kind="ExternalInput")
    onescol_h = nc.dram_tensor("onescol", [128, 1], BF16, kind="ExternalInput")
    y_h = nc.dram_tensor("y", [NB, D, L], F32, kind="ExternalOutput")

    xa_s = nc.dram_tensor("xa_s", [NB, D, L], BF16, kind="Internal")
    x2_s = nc.dram_tensor("x2_s", [NB, D, L], BF16, kind="Internal")
    xc_s = nc.dram_tensor("xc_s", [NB, E, L], BF16, kind="Internal")
    sz_s = nc.dram_tensor("sz_s", [NB, E, L], BF16, kind="Internal")
    hh_s = nc.dram_tensor("hh_s", [NB, H3, L], BF16, kind="Internal")

    dbg = {}
    if dump is not None:
        dbg["xa"] = nc.dram_tensor("dbg_xa", [D, L], BF16, kind="ExternalOutput")
        dbg["lnx"] = nc.dram_tensor("dbg_lnx", [128, DTL * L], BF16,
                                    kind="ExternalOutput")
        dbg["xc"] = nc.dram_tensor("dbg_xc", [E, L], BF16, kind="ExternalOutput")
        dbg["sz"] = nc.dram_tensor("dbg_sz", [E, L], BF16, kind="ExternalOutput")
        dbg["dbl"] = nc.dram_tensor("dbg_dbl", [RK, L], BF16, kind="ExternalOutput")
        dbg["bc"] = nc.dram_tensor("dbg_bc", [32, L], BF16, kind="ExternalOutput")
        dbg["x2"] = nc.dram_tensor("dbg_x2", [D, L], BF16, kind="ExternalOutput")
        dbg["hh"] = nc.dram_tensor("dbg_hh", [H3, L], BF16, kind="ExternalOutput")
        dbg["yg"] = nc.dram_tensor("dbg_yg", [128, ET * FC], BF16,
                                   kind="ExternalOutput")
        dbg["carry"] = nc.dram_tensor("dbg_carry", [128, ET * N], F32,
                                      kind="ExternalOutput")
    if dump == "full":
        dbg["fxa"] = nc.dram_tensor("dbg_fxa", [NB, D, L], BF16,
                                    kind="ExternalOutput")
        dbg["fx2"] = nc.dram_tensor("dbg_fx2", [NB, D, L], BF16,
                                    kind="ExternalOutput")
        dbg["fhh"] = nc.dram_tensor("dbg_fhh", [NB, H3, L], BF16,
                                    kind="ExternalOutput")

    ctx = ExitStack()
    with ctx:
        p = Prog(nc, ctx)
        blk = ctx.enter_context(nc.Block())

        def sb(name, shape, dtype):
            return ctx.enter_context(nc.sbuf_tensor(name, shape, dtype))

        sel_sb = sb("sel_sb", [32, 32 * 128], BF16)
        ones1_sb = sb("ones1_sb", [1, 128], BF16)
        ones1f_sb = sb("ones1f_sb", [1, 128], F32)
        onescol_sb = sb("onescol_sb", [128, 1], BF16)
        murow = sb("murow", [1, L], F32)
        varrow = sb("varrow", [1, L], F32)
        rstdrow = sb("rstdrow", [1, L], F32)
        murep = sb("murep", [128, L], BF16)
        rstdrep = sb("rstdrep", [128, L], F32)
        sp1 = sb("sp1", [128, FC], BF16)
        sp2 = sb("sp2", [128, FC], BF16)

        r_consts = Res(p, "consts")
        r_rows = Res(p, "rows")
        r_rep = Res(p, "rep")
        r_sp = Res(p, "sp")

        ps_a = ctx.enter_context(nc.psum_tensor("ps_a", [128, FC], F32))
        ps_b = ctx.enter_context(nc.psum_tensor("ps_b", [128, FC], F32))
        ps_dbl = ctx.enter_context(nc.psum_tensor("ps_dbl", [96, L], F32))
        r_psa = Res(p, "psa")
        r_psb = Res(p, "psb")
        r_psdbl = Res(p, "psdbl")
        r_xas = Res(p, "xa_s")
        r_x2s = Res(p, "x2_s")
        r_xcs = Res(p, "xc_s")
        r_szs = Res(p, "sz_s")
        r_hhs = Res(p, "hh_s")

        def psum_of(par):
            return (ps_a, r_psa) if (par % 2) == 0 else (ps_b, r_psb)

        def dma(eng_name, out, in_, res_w=None, res_r=None, ch="misc"):
            ch = p.chan(ch)
            p.want(eng_name, ch, p.cnt[ch])
            if res_w:
                for r in res_w:
                    r.w(eng_name)
            if res_r:
                for r in res_r:
                    r.r(eng_name)
            p.run(eng_name, lambda e: e.dma_start(out=out, in_=in_), inc=16,
                  inc_key=ch)
            if res_w:
                for r in res_w:
                    r.wdone(ch)
            if res_r:
                for r in res_r:
                    r.rdone(ch)

        constv = sb("constv", [128, 2], F32)
        sfence_sb = sb("sfence_sb", [1, 8], F32)

        def _cz(e):
            e.memset(constv[:, 0:1], EPS)
            e.memset(sfence_sb[:], 0.0)
            return e.memset(constv[:, 1:2], 1.0)
        p.run("vector", _cz)
        nc.const_aps.register(F32, EPS, constv[:, 0:1])
        nc.const_aps.register(F32, 1.0, constv[:, 1:2])

        dma("sync", sel_sb[:], sel_h.ap(), ch="w")
        dma("sync", ones1_sb[:], ones1_h.ap(), ch="w")
        dma("sync", ones1f_sb[:], ones1f_h.ap(), ch="w")
        dma("sync", onescol_sb[:], onescol_h.ap(), ch="w")
        r_consts.wdone("sync")
        nfw_sb = sb("nfw_sb", [128, DTL], F32)
        nfb_sb = sb("nfb_sb", [128, DTL], F32)
        dma("sync", nfw_sb[:], nfw_h.ap(), ch="w")
        dma("sync", nfb_sb[:], nfb_h.ap(), ch="w")
        r_nf = Res(p, "nf")
        r_nf.wdone("sync")

        def lnfence_mm(e):
            return e.matmul(ps_dbl[64:65, 0:1], onescol_sb[:],
                            onescol_sb[:, 0:1], start=True, stop=True)

        def ln_stats(src, b, xtf, xtb, xsq, r_xtf, r_xtb, r_xsq, r_src=None,
                     src_is_f32=False):
            rs = [r_src] if r_src else None
            for k in range(DTL):
                if src_is_f32:
                    dma("sync", xtf[:], src.ap()[b, k * 128:(k + 1) * 128, :],
                        res_w=[r_xtf], res_r=rs, ch="lx")

                    def cv(e):
                        return e.tensor_copy(xtb[:], xtf[:])
                    r_xtf.r("vector"); r_xtb.w("vector")
                    p.run("vector", cv)
                    r_xtf.rdone("vector"); r_xtb.wdone("vector")
                else:
                    dma("sync", xtb[:], src.ap()[b, k * 128:(k + 1) * 128, :],
                        res_w=[r_xtb], res_r=rs, ch="lx")

                def sq(e):
                    return e.tensor_tensor(xsq[:], xtb[:], xtb[:], OP.mult)
                r_xtb.r("vector"); r_xsq.w("vector")
                p.run("vector", sq)
                r_xtb.rdone("vector"); r_xsq.wdone("vector")
                vc = p.cnt["vector"]

                def mm(e, k=k):
                    last = None
                    for q in range(4):
                        sl = slice(q * 512, (q + 1) * 512)
                        last = e.matmul(ps_dbl[0:1, sl], onescol_sb[:], xtb[:, sl],
                                        start=(k == 0), stop=(k == DTL - 1))
                        last = e.matmul(ps_dbl[32:33, sl], onescol_sb[:], xsq[:, sl],
                                        start=(k == 0), stop=(k == DTL - 1))
                    return last
                p.want("tensor", "vector", vc)
                if k == 0:
                    r_psdbl.w("tensor")
                r_consts.r("tensor")
                if lnfence and k == DTL - 1:
                    p.run("tensor", mm, inc=0)
                    p.run("tensor", lnfence_mm)
                else:
                    p.run("tensor", mm)
                r_xtb.rdone("tensor")
                r_xsq.rdone("tensor")
                r_consts.rdone("tensor")
            r_psdbl.wdone("tensor")

        def ln_rows(rstd_scale=1.0):
            def vrows(e):
                e.tensor_scalar(murow[:], ps_dbl[0:1, :], 1.0 / D, None, OP.mult)
                e.tensor_scalar(varrow[:], ps_dbl[32:33, :], 1.0 / D, None, OP.mult)
                e.tensor_tensor(rstdrow[:], murow[:], murow[:], OP.mult)
                return e.tensor_tensor(varrow[:], varrow[:], rstdrow[:], OP.subtract)
            r_psdbl.r("vector")
            r_rows.w("vector")
            p.run("vector", vrows)
            r_psdbl.rdone("vector")
            vc = p.cnt["vector"]

            def srows(e):
                e.activation(varrow[:], varrow[:], AF.Ln, bias=EPS, scale=1.0)
                return e.activation(rstdrow[:], varrow[:], AF.Exp, bias=0.0,
                                    scale=-0.5 * 1.0)
            p.want("scalar", "vector", vc)
            p.run("scalar", srows)
            if rstd_scale != 1.0:
                def vs(e):
                    return e.tensor_scalar(rstdrow[:], rstdrow[:],
                                           float(rstd_scale), None, OP.mult)
                p.want("vector", "scalar", p.cnt["scalar"])
                p.run("vector", vs)
                r_rows.wdone("vector")
            else:
                r_rows.wdone("scalar")

        def ln_replicate():
            r_rep.w("vector")
            for (row, rep) in ((murow, murep), (rstdrow, rstdrep)):
                for hh in range(2):
                    ps, rps = psum_of(hh)

                    def mm(e, row=row, hh=hh, ps=ps):
                        last = None
                        for q in range(2):
                            base = hh * FC + q * 512
                            last = e.matmul(ps[:, q * 512:(q + 1) * 512],
                                            ones1f_sb[:], row[0:1, base:base + 512],
                                            start=True, stop=True)
                        return last
                    r_rows.r("tensor"); r_consts.r("tensor"); rps.w("tensor")
                    if lnfence:
                        p.run("tensor", mm, inc=0)
                        p.run("tensor", lnfence_mm)
                    else:
                        p.run("tensor", mm)
                    r_rows.rdone("tensor"); rps.wdone("tensor")

                    def cp(e, rep=rep, hh=hh, ps=ps):
                        return e.tensor_copy(rep[:, hh * FC:(hh + 1) * FC], ps[:])
                    rps.r("vector")
                    p.run("vector", cp)
                    rps.rdone("vector")
            r_rep.wdone("vector")

        def load_small_weights(l, W):
            dma("gpsimd", W["outw"][:],
                outw_h.ap()[l].rearrange("(j q) d -> q j d", q=128),
                res_w=[W["r"]], ch="w")
            dma("gpsimd", W["dtpw"][:], dtpw_h.ap()[l], ch="w")
            dma("gpsimd", W["xpw"][:],
                xpw_h.ap()[l].rearrange("(j q) r -> q j r", q=128), ch="w")
            dma("gpsimd", W["convw"][:],
                convw_h.ap()[l], ch="w")
            dma("gpsimd", W["convb"][:],
                convb_h.ap()[l], ch="w")
            dma("gpsimd", W["dtpb"][:],
                dtpb_h.ap()[l], ch="w")
            dma("gpsimd", W["dsk"][:],
                dsk_h.ap()[l], ch="w")
            dma("gpsimd", W["atmp"][:],
                alog_h.ap()[l], ch="w")
            dma("gpsimd", W["lnw"][:],
                lnw_h.ap()[l], ch="w")
            dma("gpsimd", W["lnb"][:],
                lnb_h.ap()[l], ch="w")
            gc = p.cnt[p.chan("w")]

            def sexp(e):
                return e.activation(W["A"][:], W["atmp"][:], AF.Exp, bias=0.0,
                                    scale=1.0)
            p.want("scalar", "w", gc)
            p.run("scalar", sexp)

            def vneg(e):
                return e.tensor_scalar(W["A"][:], W["A"][:], -1.0, None, OP.mult)
            p.want("vector", "scalar", p.cnt["scalar"])
            p.run("vector", vneg)
            W["r"].wdone("vector")

        def phase_pre(l, b, W, S):
            xtf, xtb, xsq = S["xtf"], S["xtb"], S["xsq"]
            r_xtf, r_xtb = S["r_xtf"], S["r_xtb"]
            if l == 0:
                ln_stats(x_h, b, xtf, xtb, xsq, r_xtf, r_xtb, S["r_xsq"], src_is_f32=True)
                ln_rows()
                ln_replicate()
                for k in range(DTL):
                    dma("sync", xtf[:], x_h.ap()[b, k * 128:(k + 1) * 128, :],
                        res_w=[r_xtf], ch="lx")

                    def ap1(e, k=k):
                        e.tensor_copy(xtb[:], xtf[:])
                        e.tensor_tensor(xsq[:], xtb[:], murep[:], OP.subtract)
                        e.tensor_tensor(xtb[:], xsq[:], rstdrep[:], OP.mult)
                        return e.tensor_tensor(xsq[:], xtb[:], xtb[:], OP.mult)
                    r_xtf.r("vector"); r_xtb.w("vector"); r_rep.r("vector")
                    S["r_xsq"].w("vector")
                    p.run("vector", ap1)
                    r_xtf.rdone("vector"); r_xtb.wdone("vector"); r_rep.rdone("vector")
                    S["r_xsq"].wdone("vector")
                    vc = p.cnt["vector"]
                    dma("sync", xa_s.ap()[b, k * 128:(k + 1) * 128, :], xtb[:],
                        res_r=[r_xtb], res_w=[r_xas], ch="sxa")

                    def mm(e, k=k):
                        last = None
                        for q in range(4):
                            sl = slice(q * 512, (q + 1) * 512)
                            last = e.matmul(ps_dbl[0:1, sl], onescol_sb[:],
                                            xtb[:, sl], start=(k == 0),
                                            stop=(k == DTL - 1))
                            last = e.matmul(ps_dbl[32:33, sl], onescol_sb[:],
                                            xsq[:, sl], start=(k == 0),
                                            stop=(k == DTL - 1))
                        return last
                    p.want("tensor", "vector", vc)
                    if k == 0:
                        r_psdbl.w("tensor")
                    if lnfence and k == DTL - 1:
                        p.run("tensor", mm, inc=0)
                        p.run("tensor", lnfence_mm)
                    else:
                        p.run("tensor", mm)
                    r_xtb.rdone("tensor")
                    S["r_xsq"].rdone("tensor")
                r_psdbl.wdone("tensor")
            else:
                ln_stats(xa_s, b, xtf, xtb, xsq, r_xtf, r_xtb, S["r_xsq"], r_src=r_xas)
            ln_rows()
            ln_replicate()
            for k in range(DTL):
                dma("sync", xtb[:], xa_s.ap()[b, k * 128:(k + 1) * 128, :],
                    res_w=[r_xtb], res_r=[r_xas], ch="lx")

                def ap2(e, k=k):
                    e.tensor_tensor(xsq[:], xtb[:], murep[:], OP.subtract)
                    return e.tensor_tensor(xsq[:], xsq[:], rstdrep[:], OP.mult)
                r_xtb.r("vector"); r_rep.r("vector"); S["r_xsq"].w("vector")
                p.run("vector", ap2)
                r_xtb.rdone("vector"); r_rep.rdone("vector")
                S["r_xsq"].wdone("vector")
                vc = p.cnt["vector"]

                def ap3(e, k=k):
                    return e.tensor_scalar(S["lnx"][:, k * L:(k + 1) * L], xsq[:],
                                           W["lnw"][:, k:k + 1],
                                           W["lnb"][:, k:k + 1],
                                           OP.mult, OP.add)
                if k == 0:
                    S["r_lnx"].w("vector")
                W["r"].r("vector")
                p.run("vector", ap3)
            S["r_lnx"].wdone("vector")

        def phase_P(l, b, W, S):
            lnx, r_lnx = S["lnx"], S["r_lnx"]
            inwh, r_inw = S["inw"], S["r_inw"]
            xcpre, r_xcpre = S["xcpre"], S["r_xcpre"]
            accb, r_accb = S["accb"], S["r_accb"]
            xcj, r_xcj = S["xcj"], S["r_xcj"]

            for zphase in range(2):
                dma("gpsimd", inwh[:],
                    inw_h.ap()[l, :, zphase * E:(zphase + 1) * E]
                    .rearrange("(k q) e -> q k e", q=128),
                    res_w=[r_inw], ch="w")
                for j in range(ET):
                    par = j % 2
                    ps, rps = psum_of(par)
                    for half in range(2):
                        def mm(e, j=j, half=half, ps=ps):
                            last = None
                            for q in range(2):
                                for k in range(DTL):
                                    last = e.matmul(
                                        ps[:, q * 512:(q + 1) * 512],
                                        inwh[:, k * E + j * 128:k * E + j * 128 + 128],
                                        lnx[:, k * L + half * FC + q * 512:
                                            k * L + half * FC + (q + 1) * 512],
                                        start=(k == 0), stop=(k == DTL - 1))
                            return last
                        r_lnx.r("tensor"); r_inw.r("tensor"); rps.w("tensor")
                        p.run("tensor", mm)
                        r_lnx.rdone("tensor"); r_inw.rdone("tensor")
                        rps.wdone("tensor")
                        tc = p.cnt["tensor"]

                        if zphase == 0:
                            def ev(e, j=j, half=half, par=par, ps=ps):
                                return e.activation(
                                    xcpre[:, 3 + half * FC:3 + (half + 1) * FC],
                                    ps[:, 0:FC], AF.Copy, bias=0.0, scale=1.0)
                            p.want("scalar", "tensor", tc)
                            if half == 0:
                                r_xcpre.w("scalar")
                            p.run("scalar", ev)
                            rps.rdone("scalar")
                            if half == 1:
                                r_xcpre.wdone("scalar")
                        else:
                            def ev(e, j=j, half=half, ps=ps):
                                return e.activation(
                                    accb[:, half * FC:(half + 1) * FC],
                                    ps[:, 0:FC], AF.Silu, bias=0.0, scale=1.0)
                            p.want("scalar", "tensor", tc)
                            if half == 0:
                                r_accb.w("scalar")
                            p.run("scalar", ev)
                            rps.rdone("scalar")
                            if half == 1:
                                r_accb.wdone("scalar")

                    if zphase == 0:
                        def c0(e, j=j):
                            return e.tensor_scalar(accb[:], xcpre[:, 3:3 + L],
                                                   W["convw"][:, j * 4 + 3:j * 4 + 4],
                                                   W["convb"][:, j:j + 1],
                                                   OP.mult, OP.add)
                        r_xcpre.r("vector"); r_accb.w("vector"); W["r"].r("vector")
                        p.run("vector", c0)
                        r_xcpre.rdone("vector")

                        def c1(e, j=j):
                            last = None
                            for kk in range(3):
                                last = e.scalar_tensor_tensor(
                                    accb[:], xcpre[:, kk:kk + L],
                                    W["convw"][:, j * 4 + kk:j * 4 + kk + 1],
                                    accb[:], OP.mult, OP.add)
                            return last
                        r_accb.w("vector")
                        p.run("vector", c1)
                        r_accb.wdone("vector")
                        r_xcpre.rdone("vector")
                        vc = p.cnt["vector"]

                        def c2(e, j=j):
                            return e.activation(xcj[:], accb[:], AF.Silu,
                                                bias=0.0, scale=1.0)
                        p.want("scalar", "vector", vc)
                        r_xcj.w("scalar")
                        p.run("scalar", c2)
                        r_accb.rdone("scalar")
                        r_xcj.wdone("scalar")
                        sc2 = p.cnt["scalar"]

                        def xp(e, j=j):
                            last = None
                            for q in range(4):
                                last = e.matmul(
                                    ps_dbl[:, q * 512:(q + 1) * 512],
                                    W["xpw"][:, j * 96:(j + 1) * 96],
                                    xcj[:, q * 512:(q + 1) * 512],
                                    start=(j == 0), stop=(j == ET - 1))
                            return last
                        p.want("tensor", "scalar", sc2)
                        if j == 0:
                            r_psdbl.w("tensor")
                        W["r"].r("tensor")
                        p.run("tensor", xp)
                        r_xcj.rdone("tensor")
                        if j == ET - 1:
                            r_psdbl.wdone("tensor")
                        dma("sync", xc_s.ap()[b, j * 128:(j + 1) * 128, :], xcj[:],
                            res_r=[r_xcj], res_w=[r_xcs], ch="sxc")
                    else:
                        dma("sync", sz_s.ap()[b, j * 128:(j + 1) * 128, :], accb[:],
                            res_r=[r_accb], res_w=[r_szs], ch="ssz")

                if zphase == 0:
                    def dev(e, b=b):
                        e.activation(W["dbl_dt"][b][:], ps_dbl[0:RK, :], AF.Copy,
                                     bias=0.0, scale=1.0)
                        return e.activation(W["bc"][b][:], ps_dbl[64:96, :],
                                            AF.Copy, bias=0.0, scale=1.0)
                    r_psdbl.r("scalar"); W["r_dbl"][b].w("scalar")
                    p.run("scalar", dev)
                    r_psdbl.rdone("scalar")
                    W["r_dbl"][b].wdone("scalar")

        class _DumpDone(Exception):
            pass

        def phase_S(l, b, W, S, dbgstop=None):
            bcrep, r_bcrep = S["bcrep"], S["r_bcrep"]
            dt_t, r_dt = S["dt_t"], S["r_dt"]
            u_t, r_u = S["u_t"], S["r_u"]
            xc_t, r_xct = S["xc_t"], S["r_xct"]
            sz_t, r_szt = S["sz_t"], S["r_szt"]
            dA_t, r_dA = S["dA_t"], S["r_dA"]
            dBu_t, r_dBu = S["dBu_t"], S["r_dBu"]
            h_t, r_h = S["h_t"], S["r_h"]
            g_t, r_g = S["g_t"], S["r_g"]
            lad, r_lad = S["lad"], S["r_lad"]
            yacc, r_yacc = S["yacc"], S["r_yacc"]
            carry, r_carry = S["carry"], S["r_carry"]
            yg, r_yg = S["yg"], S["r_yg"]
            xares, r_xares = S["xares"], S["r_xares"]
            xaout, r_xaout = S["xaout"], S["r_xaout"]

            for ch in range(CH):
                tc0 = ch * FC
                for grp in range(N // NG):
                    for i in range(2 * NG):
                        is_c = i >= NG
                        n = grp * NG + (i - NG if is_c else i)
                        selrow = (16 + n) if is_c else n
                        par = i % 2
                        ps, rps = psum_of(par)

                        def mm(e, selrow=selrow, ps=ps, b=b, tc0=tc0):
                            last = None
                            for q in range(2):
                                last = e.matmul(
                                    ps[:, q * 512:(q + 1) * 512],
                                    sel_sb[:, selrow * 128:(selrow + 1) * 128],
                                    W["bc"][b][0:32, tc0 + q * 512:tc0 + (q + 1) * 512],
                                    start=True, stop=True)
                            return last
                        W["r_dbl"][b].r("tensor"); r_consts.r("tensor")
                        rps.w("tensor")
                        p.run("tensor", mm)
                        rps.wdone("tensor")
                        tcn = p.cnt["tensor"]

                        nn = i - NG if is_c else i
                        col = (2 * nn + (1 if is_c else 0)) * FC

                        def cp(e, col=col, ps=ps):
                            return e.tensor_copy(bcrep[:, col:col + FC], ps[:, 0:FC])
                        p.want("vector", "tensor", tcn)
                        if i == 0:
                            r_bcrep.w("vector")
                        p.run("vector", cp)
                        rps.rdone("vector")
                    r_bcrep.wdone("vector")

                    for j in range(ET):
                        par = j % 2
                        ps, rps = psum_of(par)
                        dma("sync", xc_t[:],
                            xc_s.ap()[b, j * 128:(j + 1) * 128, tc0:tc0 + FC],
                            res_w=[r_xct], res_r=[r_xcs], ch="lxc")
                        if grp == 1:
                            dma("sync", sz_t[:],
                                sz_s.ap()[b, j * 128:(j + 1) * 128, tc0:tc0 + FC],
                                res_w=[r_szt], res_r=[r_szs], ch="lsz")

                        def mm(e, j=j, ps=ps, b=b, tc0=tc0):
                            last = None
                            for q in range(2):
                                last = e.matmul(
                                    ps[:, q * 512:(q + 1) * 512],
                                    W["dtpw"][:, j * 128:(j + 1) * 128],
                                    W["dbl_dt"][b][0:RK, tc0 + q * 512:tc0 + (q + 1) * 512],
                                    start=True, stop=True)
                            return last
                        W["r_dbl"][b].r("tensor"); W["r"].r("tensor")
                        rps.w("tensor")
                        p.run("tensor", mm)
                        rps.wdone("tensor")
                        tcn = p.cnt["tensor"]

                        def spl(e, j=j, ps=ps):
                            e.activation(sp1[:], ps[:, 0:FC], AF.Abs,
                                         bias=W["dtpb"][:, j:j + 1], scale=1.0)
                            e.activation(sp1[:], sp1[:], AF.Exp, bias=0.0, scale=-1.0)
                            e.activation(sp1[:], sp1[:], AF.Ln, bias=1.0, scale=1.0)
                            return e.activation(sp2[:], ps[:, 0:FC], AF.Relu,
                                                bias=W["dtpb"][:, j:j + 1], scale=1.0)
                        p.want("scalar", "tensor", tcn)
                        r_sp.w("scalar")
                        p.run("scalar", spl)
                        rps.rdone("scalar")
                        r_sp.wdone("scalar")
                        sc = p.cnt["scalar"]

                        def vdt(e, j=j, grp=grp):
                            e.tensor_tensor(dt_t[:], sp1[:], sp2[:], OP.add)
                            last = e.tensor_tensor(u_t[:], dt_t[:], xc_t[:], OP.mult)
                            if grp == 0:
                                last = e.tensor_scalar(yacc[:], xc_t[:],
                                                       W["dsk"][:, j:j + 1], None,
                                                       OP.mult)
                            return last
                        p.want("vector", "scalar", sc)
                        r_dt.w("vector"); r_u.w("vector"); r_yacc.w("vector")
                        r_xct.r("vector"); r_sp.r("vector")
                        p.run("vector", vdt)
                        r_dt.wdone("vector"); r_u.wdone("vector")
                        r_yacc.wdone("vector"); r_xct.rdone("vector")
                        r_sp.rdone("vector")

                        lad_filled = 0
                        for nn in range(NG):
                            n = grp * NG + nn
                            parA = nn % 2

                            def sda(e, j=j, n=n, parA=parA):
                                return e.activation(
                                    dA_t[parA][:], dt_t[:], AF.Exp, bias=0.0,
                                    scale=W["A"][:, j * N + n:j * N + n + 1])
                            r_dt.r("scalar"); r_dA[parA].w("scalar")
                            W["r"].r("scalar")
                            p.run("scalar", sda)
                            r_dt.rdone("scalar")
                            r_dA[parA].wdone("scalar")
                            scn = p.cnt["scalar"]

                            def vscan(e, j=j, n=n, nn=nn, ch=ch, parA=parA):
                                e.tensor_tensor(dBu_t[:], u_t[:],
                                                bcrep[:, (2 * nn) * FC:(2 * nn + 1) * FC],
                                                OP.mult)
                                init = 0.0 if ch == 0 else carry[:, j * N + n:j * N + n + 1]
                                return e.tensor_tensor_scan(h_t[:], dA_t[parA][:],
                                                            dBu_t[:], init,
                                                            OP.mult, OP.add)
                            p.want("vector", "scalar", scn)
                            r_dBu.w("vector"); r_h.w("vector")
                            r_u.r("vector"); r_bcrep.r("vector")
                            if ch == 1:
                                r_carry.r("vector")
                            p.run("vector", vscan)
                            r_dA[parA].rdone("vector")
                            r_dBu.wdone("vector"); r_h.wdone("vector")
                            r_u.rdone("vector"); r_bcrep.rdone("vector")
                            if ch == 1:
                                r_carry.rdone("vector")
                            if ch == 0:
                                vsc = p.cnt["vector"]
                                p.want("vector", "vector", 0)
                                def vcarry(e, j=j, n=n, vsc=vsc):
                                    e.wait_ge(p.sem["vector"], vsc)
                                    return e.tensor_copy(
                                        carry[:, j * N + n:j * N + n + 1],
                                        h_t[:, FC - 1:FC])
                                r_carry.w("vector")
                                p.run("vector", vcarry)
                                r_carry.wdone("vector")

                            def vgm(e, nn=nn):
                                dst = lad[0] if (nn % 2 == 0) else g_t
                                last = e.tensor_tensor(
                                    dst[:], h_t[:],
                                    bcrep[:, (2 * nn + 1) * FC:(2 * nn + 2) * FC],
                                    OP.mult)
                                if nn % 2 == 0:
                                    return last
                                k = 0
                                while (nn >> k) & 1:
                                    last = e.tensor_tensor(g_t[:], g_t[:],
                                                           lad[k][:], OP.add)
                                    k += 1
                                if nn == NG - 1:
                                    return last
                                return e.tensor_copy(lad[k][:], g_t[:])
                            r_lad.w("vector"); r_g.w("vector")
                            p.run("vector", vgm)
                            r_lad.wdone("vector"); r_g.wdone("vector")

                        if grp == 0:
                            def vstash(e, j=j):
                                return e.tensor_tensor(yg[:, j * FC:(j + 1) * FC],
                                                       yacc[:], g_t[:], OP.add)
                            r_yacc.r("vector"); r_g.r("vector")
                            if j == 0:
                                r_yg.w("vector")
                            p.run("vector", vstash)
                            r_yacc.rdone("vector"); r_g.rdone("vector")
                        else:
                            def vfin(e, j=j):
                                e.tensor_tensor(yacc[:], g_t[:],
                                                yg[:, j * FC:(j + 1) * FC], OP.add)
                                return e.tensor_tensor(yg[:, j * FC:(j + 1) * FC],
                                                       yacc[:], sz_t[:], OP.mult)
                            r_g.r("vector"); r_szt.r("vector"); r_yacc.w("vector")
                            p.run("vector", vfin)
                            r_g.rdone("vector"); r_szt.rdone("vector")
                            r_yacc.wdone("vector")
                            if j == ET - 1:
                                r_yg.wdone("vector")

                if dbgstop == f"S{l}c{ch}":
                    dump_sbuf(dbg["yg"], S["yg"])
                    dump_sbuf(dbg["carry"], S["carry"])
                    p.finalize(blk, {"sync": [(k, v) for k, v in p.cnt.items() if v > 0]})
                    raise _DumpDone

                for k in range(DTL):
                    par = k % 2
                    ps, rps = psum_of(par)

                    def mo(e, k=k, ps=ps):
                        last = None
                        for q in range(2):
                            for j in range(ET):
                                last = e.matmul(
                                    ps[:, q * 512:(q + 1) * 512],
                                    W["outw"][:, j * 1024 + k * 128:j * 1024 + k * 128 + 128],
                                    yg[:, j * FC + q * 512:j * FC + (q + 1) * 512],
                                    start=(j == 0), stop=(j == ET - 1))
                        return last
                    r_yg.r("tensor"); W["r"].r("tensor"); rps.w("tensor")
                    p.run("tensor", mo)
                    r_yg.rdone("tensor"); rps.wdone("tensor")
                    tcn = p.cnt["tensor"]

                    dma("sync", xares[:],
                        xa_s.ap()[b, k * 128:(k + 1) * 128, tc0:tc0 + FC],
                        res_w=[r_xares], res_r=[r_xas], ch="lxa")

                    def vres(e, k=k, ps=ps):
                        return e.tensor_tensor(xaout[:], ps[:, 0:FC], xares[:],
                                               OP.add)
                    p.want("vector", "tensor", tcn)
                    r_xares.r("vector"); r_xaout.w("vector")
                    p.run("vector", vres)
                    rps.rdone("vector"); r_xares.rdone("vector")
                    r_xaout.wdone("vector")
                    dma("sync", xa_s.ap()[b, k * 128:(k + 1) * 128, tc0:tc0 + FC],
                        xaout[:], res_r=[r_xaout], res_w=[r_xas], ch="sxa")

        def phase_post(b, S):
            xtf, xtb, xsq = S["xtf"], S["xtb"], S["xsq"]
            r_xtf, r_xtb = S["r_xtf"], S["r_xtb"]
            accb, r_accb = S["accb"], S["r_accb"]
            ln_stats(xa_s, b, xtf, xtb, xsq, r_xtf, r_xtb, S["r_xsq"], r_src=r_xas)
            ln_rows()
            ln_replicate()
            for k in range(DTL):
                dma("sync", xtb[:], xa_s.ap()[b, k * 128:(k + 1) * 128, :],
                    res_w=[r_xtb], res_r=[r_xas], ch="lx")
                dma("sync", xtf[:], x_h.ap()[b, k * 128:(k + 1) * 128, :],
                    res_w=[r_xtf], ch="lx0")

                def ap1(e, k=k):
                    e.tensor_tensor(xsq[:], xtb[:], murep[:], OP.subtract)
                    return e.tensor_tensor(xsq[:], xsq[:], rstdrep[:], OP.mult)
                r_xtb.r("vector"); r_rep.r("vector"); S["r_xsq"].w("vector")
                p.run("vector", ap1)
                r_xtb.rdone("vector"); r_rep.rdone("vector")
                S["r_xsq"].wdone("vector")
                vc = p.cnt["vector"]

                def ap2(e, k=k):
                    e.tensor_scalar(accb[:], xsq[:],
                                    nfw_sb[:, k:k + 1], nfb_sb[:, k:k + 1],
                                    OP.mult, OP.add)
                    return e.tensor_tensor(accb[:], accb[:], xtf[:], OP.add)
                r_nf.r("vector"); r_xtf.r("vector"); r_accb.w("vector")
                p.run("vector", ap2)
                r_xtf.rdone("vector"); r_accb.wdone("vector")
                dma("sync", x2_s.ap()[b, k * 128:(k + 1) * 128, :], accb[:],
                    res_r=[r_accb], res_w=[r_x2s], ch="sx2")

        def phase_ffn_gateup(b, S, serialize=False, fence=False, sfen=False, exp=None):
            def tfence(e):
                return e.matmul(ps_dbl[0:1, 0:1], onescol_sb[:],
                                onescol_sb[:, 0:1], start=True, stop=True)

            def sfence(e):
                return e.activation(sfence_sb[0:1, 0:8], sfence_sb[0:1, 0:8],
                                    AF.Copy, bias=0.0, scale=1.0)
            """LN(x2)/127 -> xt; gate/up/hh -> hh scratch (per FF chunk)."""
            xt = S["xt"]
            x2t = S["x2t"]
            gs, us = S["gs"], S["us"]
            hh = S["hht"]
            r_xt, r_x2t = S["r_xt"], S["r_x2t"]
            r_gu, r_hh = S["r_gu"], S["r_hh"]
            for cc in range(L // FF):
                c0 = cc * FF
                for k in range(DTL):
                    dma("sync", x2t[k][:],
                        x2_s.ap()[b, k * 128:(k + 1) * 128, c0:c0 + FF],
                        res_w=[r_x2t[k]], res_r=[r_x2s], ch=f"lx2{k % 2}")

                    def apl(e, k=k, c0=c0, exp=exp):
                        if exp == "e_xtnodma":
                            e.tensor_tensor(xt[k][:], murep[:, c0:c0 + FF],
                                            murep[:, c0:c0 + FF], OP.subtract)
                            return e.tensor_tensor(xt[k][:], xt[k][:],
                                                   rstdrep[:, c0:c0 + FF],
                                                   OP.mult)
                        e.tensor_tensor(xt[k][:], x2t[k][:],
                                        murep[:, c0:c0 + FF], OP.subtract)
                        return e.tensor_tensor(xt[k][:], xt[k][:],
                                               rstdrep[:, c0:c0 + FF], OP.mult)
                    r_x2t[k].r("vector"); r_rep.r("vector"); r_xt[k].w("vector")
                    p.run("vector", apl)
                    r_x2t[k].rdone("vector"); r_xt[k].wdone("vector")
                if exp == "e_xtlag":
                    def vfence(e):
                        return e.tensor_copy(sfence_sb[0:1, 0:8],
                                             sfence_sb[0:1, 0:8])
                    p.run("vector", vfence)
                if serialize == "chunk":
                    p.barrier_into()
                for hp in range(HPT):
                    par = hp % 2
                    ps, rps = psum_of(par)

                    def mg(e, hp=hp, ps=ps, exp=exp):
                        last = None
                        for k in range(DTL):
                            mov = murep[:, 0:FF] if exp == "e_noxt" else xt[k][:]
                            last = e.matmul(
                                ps[:, 0:FF],
                                S["wg"][:, k * H3 + hp * 128:k * H3 + hp * 128 + 128],
                                mov, start=(k == 0), stop=(k == DTL - 1))
                        return last
                    for k in range(DTL):
                        r_xt[k].r("tensor")
                    S["r_w"].r("tensor"); rps.w("tensor")
                    if fence:
                        p.run("tensor", mg, inc=0)
                        p.run("tensor", tfence)
                    else:
                        p.run("tensor", mg)
                    for k in range(DTL):
                        r_xt[k].rdone("tensor")
                    rps.wdone("tensor")
                    tc1 = p.cnt["tensor"]

                    def sg(e, ps=ps):
                        return e.activation(gs[:], ps[:, 0:FF], AF.Silu,
                                            bias=0.0, scale=1.0)
                    p.want("scalar", "tensor", tc1)
                    r_gu.w("scalar")
                    if sfen:
                        p.run("scalar", sg, inc=0)
                        p.run("scalar", sfence)
                    else:
                        p.run("scalar", sg)
                    rps.rdone("scalar")

                    def mu_(e, hp=hp, ps=ps, exp=exp):
                        last = None
                        for k in range(DTL):
                            mov = murep[:, 0:FF] if exp == "e_noxt" else xt[k][:]
                            last = e.matmul(
                                ps[:, 0:FF],
                                S["wu"][:, k * H3 + hp * 128:k * H3 + hp * 128 + 128],
                                mov, start=(k == 0), stop=(k == DTL - 1))
                        return last
                    rps.w("tensor")
                    if fence:
                        p.run("tensor", mu_, inc=0)
                        p.run("tensor", tfence)
                    else:
                        p.run("tensor", mu_)
                    rps.wdone("tensor")
                    for k in range(DTL):
                        r_xt[k].rdone("tensor")
                    tc2 = p.cnt["tensor"]

                    def su(e, ps=ps):
                        return e.activation(us[:], ps[:, 0:FF], AF.Copy,
                                            bias=0.0, scale=1.0)
                    p.want("scalar", "tensor", tc2)
                    if sfen:
                        p.run("scalar", su, inc=0)
                        p.run("scalar", sfence)
                    else:
                        p.run("scalar", su)
                    rps.rdone("scalar")
                    r_gu.wdone("scalar")
                    sc1 = p.cnt["scalar"]

                    def vh(e, exp=exp):
                        if exp == "e_nosu":
                            return e.tensor_copy(hh[:], gs[:])
                        return e.scalar_tensor_tensor(hh[:], gs[:], 1.0 / 127.0,
                                                      us[:], OP.mult, OP.mult)
                    p.want("vector", "scalar", sc1)
                    r_gu.r("vector"); r_hh.w("vector")
                    p.run("vector", vh)
                    r_gu.rdone("vector"); r_hh.wdone("vector")
                    dma("sync",
                        hh_s.ap()[b, hp * 128:(hp + 1) * 128, c0:c0 + FF],
                        hh[:], res_r=[r_hh], res_w=[r_hhs], ch="shh")
                    if serialize is True or (serialize == "h4" and hp % 4 == 3):
                        p.barrier_into()

        def phase_ffn_down(b, S):
            hh = S["hhall"]
            r_hh = S["r_hhall"]
            x2t = S["x2t"]
            r_x2t = S["r_x2t"]
            outt = S["outt"]
            r_outt = S["r_outt"]
            for cc in range(L // FF):
                c0 = cc * FF
                for hp in range(HPT):
                    dma("sync", hh[hp][:],
                        hh_s.ap()[b, hp * 128:(hp + 1) * 128, c0:c0 + FF],
                        res_w=[r_hh[hp]], res_r=[r_hhs], ch=f"lhh{hp % 4}")
                for k in range(DTL):
                    par = k % 2
                    ps, rps = psum_of(par)
                    dma("sync", x2t[k][:],
                        x2_s.ap()[b, k * 128:(k + 1) * 128, c0:c0 + FF],
                        res_w=[r_x2t[k]], res_r=[r_x2s], ch="lx2a")

                    def md(e, k=k, ps=ps):
                        last = None
                        for hp in range(HPT):
                            last = e.matmul(
                                ps[:, 0:FF],
                                S["wd"][:, hp * 1024 + k * 128:hp * 1024 + k * 128 + 128],
                                hh[hp][:], start=(hp == 0), stop=(hp == HPT - 1))
                        return last
                    for hp in range(HPT):
                        r_hh[hp].r("tensor")
                    S["r_w"].r("tensor"); rps.w("tensor")
                    p.run("tensor", md)
                    for hp in range(HPT):
                        r_hh[hp].rdone("tensor")
                    rps.wdone("tensor")
                    tcn = p.cnt["tensor"]

                    def vo(e, k=k, ps=ps):
                        return e.tensor_tensor(outt[k % 2][:], ps[:, 0:FF],
                                               x2t[k][:], OP.add)
                    p.want("vector", "tensor", tcn)
                    r_x2t[k].r("vector"); r_outt[k % 2].w("vector")
                    p.run("vector", vo)
                    rps.rdone("vector"); r_x2t[k].rdone("vector")
                    r_outt[k % 2].wdone("vector")
                    dma("sync", y_h.ap()[b, k * 128:(k + 1) * 128, c0:c0 + FF],
                        outt[k % 2][:], res_r=[r_outt[k % 2]], ch=f"sy{k % 2}")

        def dump_dram(dst, srcap):
            p.barrier_into(("sync",))
            ch = p.chan("dbg")
            p.want("sync", ch, p.cnt[ch])
            p.run("sync", lambda e, d=dst, s=srcap: e.dma_start(out=d.ap(), in_=s),
                  inc=16, inc_key=ch)

        def dump_sbuf(dst, t):
            p.barrier_into(("sync",))
            ch = p.chan("dbg")
            p.want("sync", ch, p.cnt[ch])
            p.run("sync", lambda e, d=dst, t=t: e.dma_start(out=d.ap(), in_=t[:]),
                  inc=16, inc_key=ch)

        for l in range(NL):
            with ExitStack() as lctx:
                def lsb(name, shape, dtype):
                    return lctx.enter_context(
                        nc.sbuf_tensor(f"{name}_{l}", shape, dtype))
                W = {
                    "outw": lsb("outw", [128, ET * 1024], BF16),
                    "dtpw": lsb("dtpw", [RK, E], BF16),
                    "xpw": lsb("xpw", [128, ET * 96], BF16),
                    "convw": lsb("convw", [128, ET * 4], F32),
                    "convb": lsb("convb", [128, ET], F32),
                    "dtpb": lsb("dtpb", [128, ET], F32),
                    "dsk": lsb("dsk", [128, ET], F32),
                    "atmp": lsb("atmp", [128, ET * N], F32),
                    "A": lsb("A", [128, ET * N], F32),
                    "lnw": lsb("lnw", [128, DTL], F32),
                    "lnb": lsb("lnb", [128, DTL], F32),
                    "dbl_dt": [lsb(f"dbldt{b}", [RK, L], BF16) for b in range(NB)],
                    "bc": [lsb(f"bc{b}", [32, L], BF16) for b in range(NB)],
                }
                W["r"] = Res(p, f"W{l}")
                W["r_dbl"] = [Res(p, f"dbl{l}{b}") for b in range(NB)]
                p.barrier_into(("gpsimd",))
                load_small_weights(l, W)

                with ExitStack() as pctx:
                    def psb(name, shape, dtype):
                        return pctx.enter_context(
                            nc.sbuf_tensor(f"{name}_{l}", shape, dtype))
                    S = {
                        "inw": psb("inw", [128, DTL * E], BF16),
                        "lnx": psb("lnx", [128, DTL * L], BF16),
                        "xcpre": psb("xcpre", [128, L + 4], BF16),
                        "accb": psb("accb", [128, L], BF16),
                        "xcj": psb("xcj", [128, L], BF16),
                        "xtf": psb("xtf", [128, L], F32),
                        "xtb": psb("xtb", [128, L], BF16),
                        "xsq": psb("xsq", [128, L], BF16),
                    }
                    for nm in ("inw", "lnx", "xcpre", "accb", "xcj", "xtf",
                               "xtb", "xsq"):
                        S["r_" + nm] = Res(p, nm)
                    S["r_xtf"], S["r_xtb"] = S["r_xtf"], S["r_xtb"]
                    p.barrier_into()
                    def z3(e, _xp=S["xcpre"]):
                        return e.memset(_xp[:, 0:3], 0.0)
                    p.run("vector", z3)
                    S["r_xcpre"].wdone("vector")
                    for b in range(NB):
                        phase_pre(l, b, W, S)
                        if dump == f"pre{l}" and b == 0:
                            dump_dram(dbg["xa"], xa_s.ap()[0])
                            dump_sbuf(dbg["lnx"], S["lnx"])
                            p.finalize(blk, {"sync": [(k, v) for k, v in p.cnt.items() if v > 0]})
                            return
                        phase_P(l, b, W, S)
                        if dump == f"P{l}" and b == 0:
                            dump_dram(dbg["xc"], xc_s.ap()[0])
                            dump_dram(dbg["sz"], sz_s.ap()[0])
                            dump_sbuf(dbg["dbl"], W["dbl_dt"][0])
                            dump_sbuf(dbg["bc"], W["bc"][0])
                            p.finalize(blk, {"sync": [(k, v) for k, v in p.cnt.items() if v > 0]})
                            return
                    p.barrier_into()

                with ExitStack() as sctx:
                    def ssb(name, shape, dtype):
                        return sctx.enter_context(
                            nc.sbuf_tensor(f"{name}_{l}", shape, dtype))
                    S = {
                        "bcrep": ssb("bcrep", [128, 2 * NG * FC], BF16),
                        "dt_t": ssb("dt_t", [128, FC], BF16),
                        "u_t": ssb("u_t", [128, FC], BF16),
                        "xc_t": ssb("xc_t", [128, FC], BF16),
                        "sz_t": ssb("sz_t", [128, FC], BF16),
                        "dA_t": [ssb(f"dA{i}", [128, FC], F32) for i in range(2)],
                        "dBu_t": ssb("dBu_t", [128, FC], BF16),
                        "h_t": ssb("h_t", [128, FC], BF16),
                        "g_t": ssb("g_t", [128, FC], BF16),
                        "lad": [ssb(f"lad{i}", [128, FC], BF16) for i in range(3)],
                        "yacc": ssb("yacc", [128, FC], BF16),
                        "carry": ssb("carry", [128, ET * N], F32),
                        "yg": ssb("yg", [128, ET * FC], BF16),
                        "xares": ssb("xares", [128, FC], BF16),
                        "xaout": ssb("xaout", [128, FC], BF16),
                    }
                    for nm in ("bcrep", "dt", "u", "xct", "szt", "dBu", "h",
                               "g", "lad", "yacc", "carry", "yg", "xares",
                               "xaout"):
                        S["r_" + nm] = Res(p, nm)
                    S["r_dA"] = [Res(p, f"dA{i}") for i in range(2)]
                    p.barrier_into()
                    for b in range(NB):
                        try:
                            phase_S(l, b, W, S,
                                    dbgstop=dump if b == 0 else None)
                        except _DumpDone:
                            return
                        if dump == f"S{l}" and b == 0:
                            dump_dram(dbg["xa"], xa_s.ap()[0])
                            dump_sbuf(dbg["yg"], S["yg"])
                            p.finalize(blk, {"sync": [(k, v) for k, v in p.cnt.items() if v > 0]})
                            return
                    p.barrier_into()

        with ExitStack() as qctx:
            def qsb(name, shape, dtype):
                return qctx.enter_context(nc.sbuf_tensor(name, shape, dtype))
            S = {
                "xtf": qsb("q_xtf", [128, L], F32),
                "xtb": qsb("q_xtb", [128, L], BF16),
                "xsq": qsb("q_xsq", [128, L], BF16),
                "accb": qsb("q_accb", [128, L], BF16),
            }
            for nm in ("xtf", "xtb", "xsq", "accb"):
                S["r_" + nm] = Res(p, nm)
            p.barrier_into()
            for b in range(NB):
                phase_post(b, S)
                if dump == "post" and b == 0:
                    dump_dram(dbg["x2"], x2_s.ap()[0])
                    p.finalize(blk, {"sync": [(k, v) for k, v in p.cnt.items() if v > 0]})
                    return
            p.barrier_into()

        with ExitStack() as fctx:
            def fsb(name, shape, dtype):
                return fctx.enter_context(nc.sbuf_tensor(name, shape, dtype))
            S = {
                "wg": fsb("wg_sb", [128, DTL * H3], BF16),
                "wu": fsb("wu_sb", [128, DTL * H3], BF16),
                "xt": [fsb(f"f_xt{k}", [128, FF], BF16) for k in range(DTL)],
                "x2t": [fsb(f"f_x2t{k}", [128, FF], BF16) for k in range(DTL)],
                "gs": fsb("f_gs", [128, FF], BF16),
                "us": fsb("f_us", [128, FF], BF16),
                "hht": fsb("f_hht", [128, FF], BF16),
                "xtb": fsb("f_xtb", [128, L], BF16),
                "xsq": fsb("f_xsq", [128, L], BF16),
            }
            S["r_w"] = Res(p, "fw")
            S["r_xt"] = [Res(p, f"fxt{k}") for k in range(DTL)]
            S["r_x2t"] = [Res(p, f"fx2t{k}") for k in range(DTL)]
            S["r_gu"] = Res(p, "fgu")
            S["r_hh"] = Res(p, "fhh")
            S["r_xtb"] = Res(p, "fxtb")
            S["r_xtf"] = Res(p, "fxtf")
            S["r_xsq"] = Res(p, "fxsq")
            p.barrier_into()
            dma("gpsimd", S["wg"][:],
                wg_h.ap().rearrange("(k q) h -> q k h", q=128), res_w=[S["r_w"]],
                ch="w")
            dma("gpsimd", S["wu"][:],
                wu_h.ap().rearrange("(k q) h -> q k h", q=128), ch="w")
            S["r_w"].wdone("gpsimd")
            for b in range(NB):
                ln_stats(x2_s, b, None, S["xtb"], S["xsq"], None, S["r_xtb"], S["r_xsq"],
                         r_src=r_x2s)
                ln_rows(rstd_scale=1.0 / 127.0)
                ln_replicate()
                phase_ffn_gateup(b, S, serialize=({'serf': True, 'serc': 'chunk', 'serh4': 'h4'}.get(dump, False)), fence=(dump in ('v2',)), sfen=(dump in ('v3',)), exp=dump)
                if dump == "ffnA" and b == 0:
                    dump_dram(dbg["hh"], hh_s.ap()[0])
                    p.finalize(blk, {"sync": [(k, v) for k, v in p.cnt.items() if v > 0]})
                    return
            p.barrier_into()

        with ExitStack() as gctx:
            def gsb(name, shape, dtype):
                return gctx.enter_context(nc.sbuf_tensor(name, shape, dtype))
            S2 = {
                "wd": gsb("wd_sb", [128, HPT * 1024], BF16),
                "hhall": [gsb(f"g_hh{i}", [128, FF], BF16) for i in range(HPT)],
                "x2t": [gsb(f"g_x2t{k}", [128, FF], BF16) for k in range(DTL)],
                "outt": [gsb(f"g_out{i}", [128, FF], F32) for i in range(2)],
            }
            S2["r_w"] = Res(p, "gw")
            S2["r_hhall"] = [Res(p, f"ghh{i}") for i in range(HPT)]
            S2["r_x2t"] = [Res(p, f"gx2t{k}") for k in range(DTL)]
            S2["r_outt"] = [Res(p, f"gout{i}") for i in range(2)]
            p.barrier_into()
            dma("gpsimd", S2["wd"][:],
                wd_h.ap().rearrange("(j q) d -> q j d", q=128),
                res_w=[S2["r_w"]], ch="w")
            for b in range(NB):
                phase_ffn_down(b, S2)

        if dump == "full":
            for (d, s) in (("fxa", xa_s), ("fx2", x2_s), ("fhh", hh_s)):
                p.barrier_into(("sync",))
                ch = p.chan("dbg")
                p.want("sync", ch, p.cnt[ch])
                p.run("sync", lambda e, d=dbg[d], s=s: e.dma_start(out=d.ap(),
                                                                  in_=s.ap()),
                      inc=16, inc_key=ch)

        final = {"sync": [(k, v) for k, v in p.cnt.items() if v > 0]}
        p.finalize(blk, final)
    return nc


_SEL = None


def _consts():
    global _SEL
    if _SEL is None:
        sel = np.zeros((32, 32, 128), np.float32)
        for r in range(32):
            sel[r, r, :] = 1.0
        _SEL = sel.reshape(32, 32 * 128).astype(BB)
    return _SEL


def kernel(L0_attn_x, L0_attn_ln_w, L0_attn_ln_b, L0_attn_in_proj_w,
           L0_attn_conv_w, L0_attn_conv_b, L0_attn_x_proj_w, L0_attn_dt_proj_w,
           L0_attn_dt_proj_b, L0_attn_A_log, L0_attn_D_skip, L0_attn_out_proj_w,
           L0_attn_norm_f_w, L0_attn_norm_f_b, L0_ffn_w_gate, L0_ffn_w_up,
           L0_ffn_w_down):
    x = np.asarray(L0_attn_x, np.float32)
    xT = np.ascontiguousarray(x.transpose(0, 2, 1))
    inw = np.ascontiguousarray(np.asarray(L0_attn_in_proj_w, np.float32)
                               .transpose(0, 2, 1))
    xpw = np.ascontiguousarray(np.asarray(L0_attn_x_proj_w, np.float32)
                               .transpose(0, 2, 1))
    dtpw = np.ascontiguousarray(np.asarray(L0_attn_dt_proj_w, np.float32)
                                .transpose(0, 2, 1))
    outw = np.ascontiguousarray(np.asarray(L0_attn_out_proj_w, np.float32)
                                .transpose(0, 2, 1))
    wg = np.ascontiguousarray(np.asarray(L0_ffn_w_gate, np.int32).T)
    wu = np.ascontiguousarray(np.asarray(L0_ffn_w_up, np.int32).T)
    wd = np.ascontiguousarray(np.asarray(L0_ffn_w_down, np.int32).T)

    sel = _consts()
    ones1 = np.ones((1, 128), np.float32).astype(BB)
    ones1f = np.ones((1, 128), np.float32)
    onescol = np.ones((128, 1), np.float32).astype(BB)

    def pp2(a):
        a = np.asarray(a, np.float32)
        t = a.shape[1] // 128
        return np.ascontiguousarray(a.reshape(NL, t, 128).transpose(0, 2, 1))

    def pp3(a):
        a = np.asarray(a, np.float32)
        t, k = a.shape[1] // 128, a.shape[2]
        return np.ascontiguousarray(a.reshape(NL, t, 128, k).transpose(0, 2, 1, 3))

    def pp1(a):
        a = np.asarray(a, np.float32)
        return np.ascontiguousarray(a.reshape(-1, 128).T)

    common = dict(
        lnw=pp2(L0_attn_ln_w),
        lnb=pp2(L0_attn_ln_b),
        inw=inw,
        convw=pp3(L0_attn_conv_w),
        convb=pp2(L0_attn_conv_b),
        xpw=xpw,
        dtpw=dtpw,
        dtpb=pp2(L0_attn_dt_proj_b),
        alog=pp3(L0_attn_A_log),
        dsk=pp2(L0_attn_D_skip),
        outw=outw,
        nfw=pp1(L0_attn_norm_f_w),
        nfb=pp1(L0_attn_norm_f_b),
        wg=wg, wu=wu, wd=wd,
        sel=sel, ones1=ones1, ones1f=ones1f, onescol=onescol,
    )
    in_maps = []
    for c in range(8):
        m = dict(common)
        m["x"] = np.ascontiguousarray(xT[2 * c:2 * c + 2])
        in_maps.append(m)

    nc = bass.Bass("TRN2", target_bir_lowering=False)
    build(nc)
    res = run_bass_kernel_spmd(nc, in_maps, core_ids=list(range(8)))
    outs = []
    for c in range(8):
        yc = np.asarray(res.results[c]["y"], np.float32)
        outs.append(yc.transpose(0, 2, 1))
    return np.ascontiguousarray(np.concatenate(outs, axis=0))


# version 10 (final)
# speedup vs baseline: 1.0199x; 1.0031x over previous
import numpy as np
import ml_dtypes
from contextlib import ExitStack

import concourse.bass as bass
import concourse.mybir as mybir
from concourse.bass_utils import run_bass_kernel_spmd

F32 = mybir.dt.float32
BF16 = mybir.dt.bfloat16
I32 = mybir.dt.int32
AF = mybir.ActivationFunctionType
OP = mybir.AluOpType
BB = ml_dtypes.bfloat16

L = 2048
D = 1024
E = 2048
N = 16
NG = 8
RK = 64
NL = 2
ET = 16
DTL = 8
CH = 2
FC = 1024
H3 = 3072
HPT = 24
FF = 512
NB = 2
EPS = 1e-5

ENGS = ("tensor", "vector", "scalar", "sync", "gpsimd")


class Prog:

    def __init__(self, nc, ctx):
        self.nc = nc
        self.ctx = ctx
        self.ops = {e: [] for e in ENGS}
        self.sem = {e: ctx.enter_context(nc.semaphore("sem_" + e)) for e in ENGS}
        self.cnt = {e: 0 for e in ENGS}
        self.lastwait = {e: {} for e in ENGS}
        self.pending = {e: {} for e in ENGS}

    def chan(self, name):
        if name not in self.sem:
            self.sem[name] = self.ctx.enter_context(
                self.nc.semaphore("ch_" + name))
            self.cnt[name] = 0
        return name

    def want(self, eng, on, val):
        if val <= 0:
            return
        if eng == on and eng in ("tensor", "vector", "scalar"):
            return
        p = self.pending[eng]
        p[on] = max(p.get(on, 0), val)

    def barrier_into(self, engs=ENGS):
        snap = dict(self.cnt)
        for e in engs:
            for o, v in snap.items():
                self.want(e, o, v)

    def run(self, eng, fn, inc=1, inc_key=None):
        pend = [(o, v) for o, v in sorted(self.pending[eng].items())
                if v > self.lastwait[eng].get(o, 0)]
        self.pending[eng] = {}
        for o, v in pend:
            self.lastwait[eng][o] = v
        sem = self.sem
        key = inc_key or eng
        my = self.sem[key]

        def body(e, fn=fn, pend=pend, inc=inc, my=my):
            for o, v in pend:
                e.wait_ge(sem[o], v)
            instr = fn(e)
            if inc:
                instr.then_inc(my, inc)
        self.ops[eng].append(body)
        if inc:
            self.cnt[key] += inc
        return self.cnt[key]

    def finalize(self, blk, final_waits):
        for ename, attr in (("tensor", "tensor"), ("vector", "vector"),
                            ("scalar", "scalar"), ("sync", "sync"),
                            ("gpsimd", "gpsimd")):
            ops = self.ops[ename]
            fw = final_waits.get(ename, [])

            def make(ops=ops, fw=fw):
                def bodyfn(e):
                    for f in ops:
                        f(e)
                    for o, v in fw:
                        e.wait_ge(self.sem[o], v)
                return bodyfn
            getattr(blk, attr)(make())


class Res:

    def __init__(self, p, name=""):
        self.p = p
        self.name = name
        self.prod = None
        self.readers = {}

    def r(self, eng):
        if self.prod:
            self.p.want(eng, self.prod[0], self.prod[1])

    def rdone(self, eng):
        self.readers[eng] = self.p.cnt[eng]

    def w(self, eng):
        if self.prod:
            self.p.want(eng, self.prod[0], self.prod[1])
        for o, v in self.readers.items():
            self.p.want(eng, o, v)

    def wdone(self, eng):
        self.prod = (eng, self.p.cnt[eng])
        self.readers = {}


def build(nc, dump=None, lnfence=True):
    x_h = nc.dram_tensor("x", [NB, D, L], F32, kind="ExternalInput")
    lnw_h = nc.dram_tensor("lnw", [NL, 128, DTL], F32, kind="ExternalInput")
    lnb_h = nc.dram_tensor("lnb", [NL, 128, DTL], F32, kind="ExternalInput")
    inw_h = nc.dram_tensor("inw", [NL, D, 2 * E], F32, kind="ExternalInput")
    convw_h = nc.dram_tensor("convw", [NL, 128, ET, 4], F32, kind="ExternalInput")
    convb_h = nc.dram_tensor("convb", [NL, 128, ET], F32, kind="ExternalInput")
    xpw_h = nc.dram_tensor("xpw", [NL, E, 96], F32, kind="ExternalInput")
    dtpw_h = nc.dram_tensor("dtpw", [NL, RK, E], F32, kind="ExternalInput")
    dtpb_h = nc.dram_tensor("dtpb", [NL, 128, ET], F32, kind="ExternalInput")
    alog_h = nc.dram_tensor("alog", [NL, 128, ET, N], F32, kind="ExternalInput")
    dsk_h = nc.dram_tensor("dsk", [NL, 128, ET], F32, kind="ExternalInput")
    outw_h = nc.dram_tensor("outw", [NL, E, D], F32, kind="ExternalInput")
    nfw_h = nc.dram_tensor("nfw", [128, DTL], F32, kind="ExternalInput")
    nfb_h = nc.dram_tensor("nfb", [128, DTL], F32, kind="ExternalInput")
    wg_h = nc.dram_tensor("wg", [D, H3], I32, kind="ExternalInput")
    wu_h = nc.dram_tensor("wu", [D, H3], I32, kind="ExternalInput")
    wd_h = nc.dram_tensor("wd", [H3, D], I32, kind="ExternalInput")
    sel_h = nc.dram_tensor("sel", [32, 32 * 128], BF16, kind="ExternalInput")
    ones1_h = nc.dram_tensor("ones1", [1, 128], BF16, kind="ExternalInput")
    ones1f_h = nc.dram_tensor("ones1f", [1, 128], F32, kind="ExternalInput")
    onescol_h = nc.dram_tensor("onescol", [128, 1], BF16, kind="ExternalInput")
    y_h = nc.dram_tensor("y", [NB, D, L], F32, kind="ExternalOutput")

    xa_s = nc.dram_tensor("xa_s", [NB, D, L], BF16, kind="Internal")
    x2_s = nc.dram_tensor("x2_s", [NB, D, L], BF16, kind="Internal")
    xc_s = nc.dram_tensor("xc_s", [NB, E, L], BF16, kind="Internal")
    sz_s = nc.dram_tensor("sz_s", [NB, E, L], BF16, kind="Internal")
    hh_s = nc.dram_tensor("hh_s", [NB, H3, L], BF16, kind="Internal")

    dbg = {}
    if dump is not None:
        dbg["xa"] = nc.dram_tensor("dbg_xa", [D, L], BF16, kind="ExternalOutput")
        dbg["lnx"] = nc.dram_tensor("dbg_lnx", [128, DTL * L], BF16,
                                    kind="ExternalOutput")
        dbg["xc"] = nc.dram_tensor("dbg_xc", [E, L], BF16, kind="ExternalOutput")
        dbg["sz"] = nc.dram_tensor("dbg_sz", [E, L], BF16, kind="ExternalOutput")
        dbg["dbl"] = nc.dram_tensor("dbg_dbl", [RK, L], BF16, kind="ExternalOutput")
        dbg["bc"] = nc.dram_tensor("dbg_bc", [32, L], BF16, kind="ExternalOutput")
        dbg["x2"] = nc.dram_tensor("dbg_x2", [D, L], BF16, kind="ExternalOutput")
        dbg["hh"] = nc.dram_tensor("dbg_hh", [H3, L], BF16, kind="ExternalOutput")
        dbg["yg"] = nc.dram_tensor("dbg_yg", [128, ET * FC], BF16,
                                   kind="ExternalOutput")
        dbg["carry"] = nc.dram_tensor("dbg_carry", [128, ET * N], F32,
                                      kind="ExternalOutput")
    if dump == "full":
        dbg["fxa"] = nc.dram_tensor("dbg_fxa", [NB, D, L], BF16,
                                    kind="ExternalOutput")
        dbg["fx2"] = nc.dram_tensor("dbg_fx2", [NB, D, L], BF16,
                                    kind="ExternalOutput")
        dbg["fhh"] = nc.dram_tensor("dbg_fhh", [NB, H3, L], BF16,
                                    kind="ExternalOutput")

    ctx = ExitStack()
    with ctx:
        p = Prog(nc, ctx)
        blk = ctx.enter_context(nc.Block())

        def sb(name, shape, dtype):
            return ctx.enter_context(nc.sbuf_tensor(name, shape, dtype))

        sel_sb = sb("sel_sb", [32, 32 * 128], BF16)
        ones1_sb = sb("ones1_sb", [1, 128], BF16)
        ones1f_sb = sb("ones1f_sb", [1, 128], F32)
        onescol_sb = sb("onescol_sb", [128, 1], BF16)
        murow = sb("murow", [1, L], F32)
        varrow = sb("varrow", [1, L], F32)
        rstdrow = sb("rstdrow", [1, L], F32)
        murep = sb("murep", [128, L], BF16)
        rstdrep = sb("rstdrep", [128, L], F32)
        sp1 = sb("sp1", [128, FC], BF16)
        sp2 = sb("sp2", [128, FC], BF16)

        r_consts = Res(p, "consts")
        r_rows = Res(p, "rows")
        r_rep = Res(p, "rep")
        r_sp = Res(p, "sp")

        ps_a = ctx.enter_context(nc.psum_tensor("ps_a", [128, FC], F32))
        ps_b = ctx.enter_context(nc.psum_tensor("ps_b", [128, FC], F32))
        ps_dbl = ctx.enter_context(nc.psum_tensor("ps_dbl", [96, L], F32))
        r_psa = Res(p, "psa")
        r_psb = Res(p, "psb")
        r_psdbl = Res(p, "psdbl")
        r_xas = Res(p, "xa_s")
        r_x2s = Res(p, "x2_s")
        r_xcs = Res(p, "xc_s")
        r_szs = Res(p, "sz_s")
        r_hhs = Res(p, "hh_s")

        def psum_of(par):
            return (ps_a, r_psa) if (par % 2) == 0 else (ps_b, r_psb)

        def dma(eng_name, out, in_, res_w=None, res_r=None, ch="misc"):
            ch = p.chan(ch)
            p.want(eng_name, ch, p.cnt[ch])
            if res_w:
                for r in res_w:
                    r.w(eng_name)
            if res_r:
                for r in res_r:
                    r.r(eng_name)
            p.run(eng_name, lambda e: e.dma_start(out=out, in_=in_), inc=16,
                  inc_key=ch)
            if res_w:
                for r in res_w:
                    r.wdone(ch)
            if res_r:
                for r in res_r:
                    r.rdone(ch)

        constv = sb("constv", [128, 2], F32)
        sfence_sb = sb("sfence_sb", [1, 8], F32)

        def _cz(e):
            e.memset(constv[:, 0:1], EPS)
            e.memset(sfence_sb[:], 0.0)
            return e.memset(constv[:, 1:2], 1.0)
        p.run("vector", _cz)
        nc.const_aps.register(F32, EPS, constv[:, 0:1])
        nc.const_aps.register(F32, 1.0, constv[:, 1:2])

        dma("sync", sel_sb[:], sel_h.ap(), ch="w")
        dma("sync", ones1_sb[:], ones1_h.ap(), ch="w")
        dma("sync", ones1f_sb[:], ones1f_h.ap(), ch="w")
        dma("sync", onescol_sb[:], onescol_h.ap(), ch="w")
        r_consts.wdone("sync")
        nfw_sb = sb("nfw_sb", [128, DTL], F32)
        nfb_sb = sb("nfb_sb", [128, DTL], F32)
        dma("sync", nfw_sb[:], nfw_h.ap(), ch="w")
        dma("sync", nfb_sb[:], nfb_h.ap(), ch="w")
        r_nf = Res(p, "nf")
        r_nf.wdone("sync")

        def lnfence_mm(e):
            return e.matmul(ps_dbl[64:65, 0:1], onescol_sb[:],
                            onescol_sb[:, 0:1], start=True, stop=True)

        def ln_stats(src, b, xtf, xtb, xsq, r_xtf, r_xtb, r_xsq, r_src=None,
                     src_is_f32=False):
            rs = [r_src] if r_src else None
            for k in range(DTL):
                if src_is_f32:
                    dma("sync", xtf[:], src.ap()[b, k * 128:(k + 1) * 128, :],
                        res_w=[r_xtf], res_r=rs, ch="lx")

                    def cv(e):
                        return e.tensor_copy(xtb[:], xtf[:])
                    r_xtf.r("vector"); r_xtb.w("vector")
                    p.run("vector", cv)
                    r_xtf.rdone("vector"); r_xtb.wdone("vector")
                else:
                    dma("sync", xtb[:], src.ap()[b, k * 128:(k + 1) * 128, :],
                        res_w=[r_xtb], res_r=rs, ch="lx")

                def sq(e):
                    return e.tensor_tensor(xsq[:], xtb[:], xtb[:], OP.mult)
                r_xtb.r("vector"); r_xsq.w("vector")
                p.run("vector", sq)
                r_xtb.rdone("vector"); r_xsq.wdone("vector")
                vc = p.cnt["vector"]

                def mm(e, k=k):
                    last = None
                    for q in range(4):
                        sl = slice(q * 512, (q + 1) * 512)
                        last = e.matmul(ps_dbl[0:1, sl], onescol_sb[:], xtb[:, sl],
                                        start=(k == 0), stop=(k == DTL - 1))
                        last = e.matmul(ps_dbl[32:33, sl], onescol_sb[:], xsq[:, sl],
                                        start=(k == 0), stop=(k == DTL - 1))
                    return last
                p.want("tensor", "vector", vc)
                if k == 0:
                    r_psdbl.w("tensor")
                r_consts.r("tensor")
                if lnfence and k == DTL - 1:
                    p.run("tensor", mm, inc=0)
                    p.run("tensor", lnfence_mm)
                else:
                    p.run("tensor", mm)
                r_xtb.rdone("tensor")
                r_xsq.rdone("tensor")
                r_consts.rdone("tensor")
            r_psdbl.wdone("tensor")

        def ln_rows(rstd_scale=1.0):
            def vrows(e):
                e.tensor_scalar(murow[:], ps_dbl[0:1, :], 1.0 / D, None, OP.mult)
                e.tensor_scalar(varrow[:], ps_dbl[32:33, :], 1.0 / D, None, OP.mult)
                e.tensor_tensor(rstdrow[:], murow[:], murow[:], OP.mult)
                return e.tensor_tensor(varrow[:], varrow[:], rstdrow[:], OP.subtract)
            r_psdbl.r("vector")
            r_rows.w("vector")
            p.run("vector", vrows)
            r_psdbl.rdone("vector")
            vc = p.cnt["vector"]

            def srows(e):
                e.activation(varrow[:], varrow[:], AF.Ln, bias=EPS, scale=1.0)
                return e.activation(rstdrow[:], varrow[:], AF.Exp, bias=0.0,
                                    scale=-0.5 * 1.0)
            p.want("scalar", "vector", vc)
            p.run("scalar", srows)
            if rstd_scale != 1.0:
                def vs(e):
                    return e.tensor_scalar(rstdrow[:], rstdrow[:],
                                           float(rstd_scale), None, OP.mult)
                p.want("vector", "scalar", p.cnt["scalar"])
                p.run("vector", vs)
                r_rows.wdone("vector")
            else:
                r_rows.wdone("scalar")

        def ln_replicate():
            r_rep.w("vector")
            for (row, rep) in ((murow, murep), (rstdrow, rstdrep)):
                for hh in range(2):
                    ps, rps = psum_of(hh)

                    def mm(e, row=row, hh=hh, ps=ps):
                        last = None
                        for q in range(2):
                            base = hh * FC + q * 512
                            last = e.matmul(ps[:, q * 512:(q + 1) * 512],
                                            ones1f_sb[:], row[0:1, base:base + 512],
                                            start=True, stop=True)
                        return last
                    r_rows.r("tensor"); r_consts.r("tensor"); rps.w("tensor")
                    if lnfence:
                        p.run("tensor", mm, inc=0)
                        p.run("tensor", lnfence_mm)
                    else:
                        p.run("tensor", mm)
                    r_rows.rdone("tensor"); rps.wdone("tensor")

                    def cp(e, rep=rep, hh=hh, ps=ps):
                        return e.tensor_copy(rep[:, hh * FC:(hh + 1) * FC], ps[:])
                    rps.r("vector")
                    p.run("vector", cp)
                    rps.rdone("vector")
            r_rep.wdone("vector")

        def load_small_weights(l, W):
            dma("gpsimd", W["outw"][:],
                outw_h.ap()[l].rearrange("(j q) d -> q j d", q=128),
                res_w=[W["r"]], ch="w")
            dma("gpsimd", W["dtpw"][:], dtpw_h.ap()[l], ch="w")
            dma("gpsimd", W["xpw"][:],
                xpw_h.ap()[l].rearrange("(j q) r -> q j r", q=128), ch="w")
            dma("gpsimd", W["convw"][:],
                convw_h.ap()[l], ch="w")
            dma("gpsimd", W["convb"][:],
                convb_h.ap()[l], ch="w")
            dma("gpsimd", W["dtpb"][:],
                dtpb_h.ap()[l], ch="w")
            dma("gpsimd", W["dsk"][:],
                dsk_h.ap()[l], ch="w")
            dma("gpsimd", W["atmp"][:],
                alog_h.ap()[l], ch="w")
            dma("gpsimd", W["lnw"][:],
                lnw_h.ap()[l], ch="w")
            dma("gpsimd", W["lnb"][:],
                lnb_h.ap()[l], ch="w")
            gc = p.cnt[p.chan("w")]

            def sexp(e):
                return e.activation(W["A"][:], W["atmp"][:], AF.Exp, bias=0.0,
                                    scale=1.0)
            p.want("scalar", "w", gc)
            p.run("scalar", sexp)

            def vneg(e):
                return e.tensor_scalar(W["A"][:], W["A"][:], -1.0, None, OP.mult)
            p.want("vector", "scalar", p.cnt["scalar"])
            p.run("vector", vneg)
            W["r"].wdone("vector")

        def phase_pre(l, b, W, S):
            xtf, xtb, xsq = S["xtf"], S["xtb"], S["xsq"]
            r_xtf, r_xtb = S["r_xtf"], S["r_xtb"]
            if l == 0:
                ln_stats(x_h, b, xtf, xtb, xsq, r_xtf, r_xtb, S["r_xsq"], src_is_f32=True)
                ln_rows()
                ln_replicate()
                for k in range(DTL):
                    dma("sync", xtf[:], x_h.ap()[b, k * 128:(k + 1) * 128, :],
                        res_w=[r_xtf], ch="lx")

                    def ap1(e, k=k):
                        e.tensor_copy(xtb[:], xtf[:])
                        e.tensor_tensor(xsq[:], xtb[:], murep[:], OP.subtract)
                        e.tensor_tensor(xtb[:], xsq[:], rstdrep[:], OP.mult)
                        return e.tensor_tensor(xsq[:], xtb[:], xtb[:], OP.mult)
                    r_xtf.r("vector"); r_xtb.w("vector"); r_rep.r("vector")
                    S["r_xsq"].w("vector")
                    p.run("vector", ap1)
                    r_xtf.rdone("vector"); r_xtb.wdone("vector"); r_rep.rdone("vector")
                    S["r_xsq"].wdone("vector")
                    vc = p.cnt["vector"]
                    dma("sync", xa_s.ap()[b, k * 128:(k + 1) * 128, :], xtb[:],
                        res_r=[r_xtb], res_w=[r_xas], ch="sxa")

                    def mm(e, k=k):
                        last = None
                        for q in range(4):
                            sl = slice(q * 512, (q + 1) * 512)
                            last = e.matmul(ps_dbl[0:1, sl], onescol_sb[:],
                                            xtb[:, sl], start=(k == 0),
                                            stop=(k == DTL - 1))
                            last = e.matmul(ps_dbl[32:33, sl], onescol_sb[:],
                                            xsq[:, sl], start=(k == 0),
                                            stop=(k == DTL - 1))
                        return last
                    p.want("tensor", "vector", vc)
                    if k == 0:
                        r_psdbl.w("tensor")
                    if lnfence and k == DTL - 1:
                        p.run("tensor", mm, inc=0)
                        p.run("tensor", lnfence_mm)
                    else:
                        p.run("tensor", mm)
                    r_xtb.rdone("tensor")
                    S["r_xsq"].rdone("tensor")
                r_psdbl.wdone("tensor")
            else:
                ln_stats(xa_s, b, xtf, xtb, xsq, r_xtf, r_xtb, S["r_xsq"], r_src=r_xas)
            ln_rows()
            ln_replicate()
            for k in range(DTL):
                dma("sync", xtb[:], xa_s.ap()[b, k * 128:(k + 1) * 128, :],
                    res_w=[r_xtb], res_r=[r_xas], ch="lx")

                def ap2(e, k=k):
                    e.tensor_tensor(xsq[:], xtb[:], murep[:], OP.subtract)
                    return e.tensor_tensor(xsq[:], xsq[:], rstdrep[:], OP.mult)
                r_xtb.r("vector"); r_rep.r("vector"); S["r_xsq"].w("vector")
                p.run("vector", ap2)
                r_xtb.rdone("vector"); r_rep.rdone("vector")
                S["r_xsq"].wdone("vector")
                vc = p.cnt["vector"]

                def ap3(e, k=k):
                    return e.tensor_scalar(S["lnx"][:, k * L:(k + 1) * L], xsq[:],
                                           W["lnw"][:, k:k + 1],
                                           W["lnb"][:, k:k + 1],
                                           OP.mult, OP.add)
                if k == 0:
                    S["r_lnx"].w("vector")
                W["r"].r("vector")
                p.run("vector", ap3)
            S["r_lnx"].wdone("vector")

        def phase_P(l, b, W, S):
            lnx, r_lnx = S["lnx"], S["r_lnx"]
            inwh, r_inw = S["inw"], S["r_inw"]
            xcpre, r_xcpre = S["xcpre"], S["r_xcpre"]
            accb, r_accb = S["accb"], S["r_accb"]
            xcj, r_xcj = S["xcj"], S["r_xcj"]

            for zphase in range(2):
                dma("gpsimd", inwh[:],
                    inw_h.ap()[l, :, zphase * E:(zphase + 1) * E]
                    .rearrange("(k q) e -> q k e", q=128),
                    res_w=[r_inw], ch="w")
                for j in range(ET):
                    par = j % 2
                    ps, rps = psum_of(par)
                    for half in range(2):
                        def mm(e, j=j, half=half, ps=ps):
                            last = None
                            for q in range(2):
                                for k in range(DTL):
                                    last = e.matmul(
                                        ps[:, q * 512:(q + 1) * 512],
                                        inwh[:, k * E + j * 128:k * E + j * 128 + 128],
                                        lnx[:, k * L + half * FC + q * 512:
                                            k * L + half * FC + (q + 1) * 512],
                                        start=(k == 0), stop=(k == DTL - 1))
                            return last
                        r_lnx.r("tensor"); r_inw.r("tensor"); rps.w("tensor")
                        p.run("tensor", mm)
                        r_lnx.rdone("tensor"); r_inw.rdone("tensor")
                        rps.wdone("tensor")
                        tc = p.cnt["tensor"]

                        if zphase == 0:
                            def ev(e, j=j, half=half, par=par, ps=ps):
                                return e.activation(
                                    xcpre[:, 3 + half * FC:3 + (half + 1) * FC],
                                    ps[:, 0:FC], AF.Copy, bias=0.0, scale=1.0)
                            p.want("scalar", "tensor", tc)
                            if half == 0:
                                r_xcpre.w("scalar")
                            p.run("scalar", ev)
                            rps.rdone("scalar")
                            if half == 1:
                                r_xcpre.wdone("scalar")
                        else:
                            def ev(e, j=j, half=half, ps=ps):
                                return e.activation(
                                    accb[:, half * FC:(half + 1) * FC],
                                    ps[:, 0:FC], AF.Silu, bias=0.0, scale=1.0)
                            p.want("scalar", "tensor", tc)
                            if half == 0:
                                r_accb.w("scalar")
                            p.run("scalar", ev)
                            rps.rdone("scalar")
                            if half == 1:
                                r_accb.wdone("scalar")

                    if zphase == 0:
                        def c0(e, j=j):
                            return e.tensor_scalar(accb[:], xcpre[:, 3:3 + L],
                                                   W["convw"][:, j * 4 + 3:j * 4 + 4],
                                                   W["convb"][:, j:j + 1],
                                                   OP.mult, OP.add)
                        r_xcpre.r("vector"); r_accb.w("vector"); W["r"].r("vector")
                        p.run("vector", c0)
                        r_xcpre.rdone("vector")

                        def c1(e, j=j):
                            last = None
                            for kk in range(3):
                                last = e.scalar_tensor_tensor(
                                    accb[:], xcpre[:, kk:kk + L],
                                    W["convw"][:, j * 4 + kk:j * 4 + kk + 1],
                                    accb[:], OP.mult, OP.add)
                            return last
                        r_accb.w("vector")
                        p.run("vector", c1)
                        r_accb.wdone("vector")
                        r_xcpre.rdone("vector")
                        vc = p.cnt["vector"]

                        def c2(e, j=j):
                            return e.activation(xcj[:], accb[:], AF.Silu,
                                                bias=0.0, scale=1.0)
                        p.want("scalar", "vector", vc)
                        r_xcj.w("scalar")
                        p.run("scalar", c2)
                        r_accb.rdone("scalar")
                        r_xcj.wdone("scalar")
                        sc2 = p.cnt["scalar"]

                        def xp(e, j=j):
                            last = None
                            for q in range(4):
                                last = e.matmul(
                                    ps_dbl[:, q * 512:(q + 1) * 512],
                                    W["xpw"][:, j * 96:(j + 1) * 96],
                                    xcj[:, q * 512:(q + 1) * 512],
                                    start=(j == 0), stop=(j == ET - 1))
                            return last
                        p.want("tensor", "scalar", sc2)
                        if j == 0:
                            r_psdbl.w("tensor")
                        W["r"].r("tensor")
                        p.run("tensor", xp)
                        r_xcj.rdone("tensor")
                        if j == ET - 1:
                            r_psdbl.wdone("tensor")
                        dma("sync", xc_s.ap()[b, j * 128:(j + 1) * 128, :], xcj[:],
                            res_r=[r_xcj], res_w=[r_xcs], ch="sxc")
                    else:
                        dma("sync", sz_s.ap()[b, j * 128:(j + 1) * 128, :], accb[:],
                            res_r=[r_accb], res_w=[r_szs], ch="ssz")

                if zphase == 0:
                    def dev(e, b=b):
                        e.activation(W["dbl_dt"][b][:], ps_dbl[0:RK, :], AF.Copy,
                                     bias=0.0, scale=1.0)
                        return e.activation(W["bc"][b][:], ps_dbl[64:96, :],
                                            AF.Copy, bias=0.0, scale=1.0)
                    r_psdbl.r("scalar"); W["r_dbl"][b].w("scalar")
                    p.run("scalar", dev)
                    r_psdbl.rdone("scalar")
                    W["r_dbl"][b].wdone("scalar")

        class _DumpDone(Exception):
            pass

        def phase_S(l, b, W, S, dbgstop=None):
            bcrep, r_bcrep = S["bcrep"], S["r_bcrep"]
            dt_t, r_dt = S["dt_t"], S["r_dt"]
            u_t, r_u = S["u_t"], S["r_u"]
            xc_t, r_xct = S["xc_t"], S["r_xct"]
            sz_t, r_szt = S["sz_t"], S["r_szt"]
            dA_t, r_dA = S["dA_t"], S["r_dA"]
            dBu_t, r_dBu = S["dBu_t"], S["r_dBu"]
            h_t, r_h = S["h_t"], S["r_h"]
            g_t, r_g = S["g_t"], S["r_g"]
            lad, r_lad = S["lad"], S["r_lad"]
            yacc, r_yacc = S["yacc"], S["r_yacc"]
            carry, r_carry = S["carry"], S["r_carry"]
            yg, r_yg = S["yg"], S["r_yg"]
            xares, r_xares = S["xares"], S["r_xares"]
            xaout, r_xaout = S["xaout"], S["r_xaout"]

            for ch in range(CH):
                tc0 = ch * FC
                for grp in range(N // NG):
                    for i in range(2 * NG):
                        is_c = i >= NG
                        n = grp * NG + (i - NG if is_c else i)
                        selrow = (16 + n) if is_c else n
                        par = i % 2
                        ps, rps = psum_of(par)

                        def mm(e, selrow=selrow, ps=ps, b=b, tc0=tc0):
                            last = None
                            for q in range(2):
                                last = e.matmul(
                                    ps[:, q * 512:(q + 1) * 512],
                                    sel_sb[:, selrow * 128:(selrow + 1) * 128],
                                    W["bc"][b][0:32, tc0 + q * 512:tc0 + (q + 1) * 512],
                                    start=True, stop=True)
                            return last
                        W["r_dbl"][b].r("tensor"); r_consts.r("tensor")
                        rps.w("tensor")
                        p.run("tensor", mm)
                        rps.wdone("tensor")
                        tcn = p.cnt["tensor"]

                        nn = i - NG if is_c else i
                        col = (2 * nn + (1 if is_c else 0)) * FC

                        def cp(e, col=col, ps=ps):
                            return e.activation(bcrep[:, col:col + FC],
                                                ps[:, 0:FC], AF.Copy,
                                                bias=0.0, scale=1.0)
                        p.want("scalar", "tensor", tcn)
                        if i == 0:
                            r_bcrep.w("scalar")
                        p.run("scalar", cp)
                        rps.rdone("scalar")
                    r_bcrep.wdone("scalar")

                    for j in range(ET):
                        par = j % 2
                        ps, rps = psum_of(par)
                        dma("sync", xc_t[:],
                            xc_s.ap()[b, j * 128:(j + 1) * 128, tc0:tc0 + FC],
                            res_w=[r_xct], res_r=[r_xcs], ch="lxc")
                        if grp == 1:
                            dma("sync", sz_t[:],
                                sz_s.ap()[b, j * 128:(j + 1) * 128, tc0:tc0 + FC],
                                res_w=[r_szt], res_r=[r_szs], ch="lsz")

                        def mm(e, j=j, ps=ps, b=b, tc0=tc0):
                            last = None
                            for q in range(2):
                                last = e.matmul(
                                    ps[:, q * 512:(q + 1) * 512],
                                    W["dtpw"][:, j * 128:(j + 1) * 128],
                                    W["dbl_dt"][b][0:RK, tc0 + q * 512:tc0 + (q + 1) * 512],
                                    start=True, stop=True)
                            return last
                        W["r_dbl"][b].r("tensor"); W["r"].r("tensor")
                        rps.w("tensor")
                        p.run("tensor", mm)
                        rps.wdone("tensor")
                        tcn = p.cnt["tensor"]

                        def spl(e, j=j, ps=ps):
                            e.activation(sp1[:], ps[:, 0:FC], AF.Abs,
                                         bias=W["dtpb"][:, j:j + 1], scale=1.0)
                            e.activation(sp1[:], sp1[:], AF.Exp, bias=0.0, scale=-1.0)
                            e.activation(sp1[:], sp1[:], AF.Ln, bias=1.0, scale=1.0)
                            return e.activation(sp2[:], ps[:, 0:FC], AF.Relu,
                                                bias=W["dtpb"][:, j:j + 1], scale=1.0)
                        p.want("scalar", "tensor", tcn)
                        r_sp.w("scalar")
                        p.run("scalar", spl)
                        rps.rdone("scalar")
                        r_sp.wdone("scalar")
                        sc = p.cnt["scalar"]

                        def vdt(e, j=j, grp=grp):
                            e.tensor_tensor(dt_t[:], sp1[:], sp2[:], OP.add)
                            last = e.tensor_tensor(u_t[:], dt_t[:], xc_t[:], OP.mult)
                            if grp == 0:
                                last = e.tensor_scalar(yacc[:], xc_t[:],
                                                       W["dsk"][:, j:j + 1], None,
                                                       OP.mult)
                            return last
                        p.want("vector", "scalar", sc)
                        r_dt.w("vector"); r_u.w("vector"); r_yacc.w("vector")
                        r_xct.r("vector"); r_sp.r("vector")
                        p.run("vector", vdt)
                        r_dt.wdone("vector"); r_u.wdone("vector")
                        r_yacc.wdone("vector"); r_xct.rdone("vector")
                        r_sp.rdone("vector")

                        lad_filled = 0
                        for nn in range(NG):
                            n = grp * NG + nn
                            parA = nn % 2

                            def sda(e, j=j, n=n, parA=parA):
                                return e.activation(
                                    dA_t[parA][:], dt_t[:], AF.Exp, bias=0.0,
                                    scale=W["A"][:, j * N + n:j * N + n + 1])
                            r_dt.r("scalar"); r_dA[parA].w("scalar")
                            W["r"].r("scalar")
                            p.run("scalar", sda)
                            r_dt.rdone("scalar")
                            r_dA[parA].wdone("scalar")
                            scn = p.cnt["scalar"]

                            def vscan(e, j=j, n=n, nn=nn, ch=ch, parA=parA):
                                e.tensor_tensor(dBu_t[:], u_t[:],
                                                bcrep[:, (2 * nn) * FC:(2 * nn + 1) * FC],
                                                OP.mult)
                                init = 0.0 if ch == 0 else carry[:, j * N + n:j * N + n + 1]
                                return e.tensor_tensor_scan(h_t[:], dA_t[parA][:],
                                                            dBu_t[:], init,
                                                            OP.mult, OP.add)
                            p.want("vector", "scalar", scn)
                            r_dBu.w("vector"); r_h.w("vector")
                            r_u.r("vector"); r_bcrep.r("vector")
                            if ch == 1:
                                r_carry.r("vector")
                            p.run("vector", vscan)
                            r_dA[parA].rdone("vector")
                            r_dBu.wdone("vector"); r_h.wdone("vector")
                            r_u.rdone("vector"); r_bcrep.rdone("vector")
                            if ch == 1:
                                r_carry.rdone("vector")
                            if ch == 0:
                                vsc = p.cnt["vector"]
                                p.want("vector", "vector", 0)
                                def vcarry(e, j=j, n=n, vsc=vsc):
                                    e.wait_ge(p.sem["vector"], vsc)
                                    return e.tensor_copy(
                                        carry[:, j * N + n:j * N + n + 1],
                                        h_t[:, FC - 1:FC])
                                r_carry.w("vector")
                                p.run("vector", vcarry)
                                r_carry.wdone("vector")

                            def vgm(e, nn=nn):
                                dst = lad[0] if (nn % 2 == 0) else g_t
                                last = e.tensor_tensor(
                                    dst[:], h_t[:],
                                    bcrep[:, (2 * nn + 1) * FC:(2 * nn + 2) * FC],
                                    OP.mult)
                                if nn % 2 == 0:
                                    return last
                                k = 0
                                while (nn >> k) & 1:
                                    last = e.tensor_tensor(g_t[:], g_t[:],
                                                           lad[k][:], OP.add)
                                    k += 1
                                if nn == NG - 1:
                                    return last
                                return e.tensor_copy(lad[k][:], g_t[:])
                            r_lad.w("vector"); r_g.w("vector")
                            p.run("vector", vgm)
                            r_lad.wdone("vector"); r_g.wdone("vector")

                        if grp == 0:
                            def vstash(e, j=j):
                                return e.tensor_tensor(yg[:, j * FC:(j + 1) * FC],
                                                       yacc[:], g_t[:], OP.add)
                            r_yacc.r("vector"); r_g.r("vector")
                            if j == 0:
                                r_yg.w("vector")
                            p.run("vector", vstash)
                            r_yacc.rdone("vector"); r_g.rdone("vector")
                        else:
                            def vfin(e, j=j):
                                e.tensor_tensor(yacc[:], g_t[:],
                                                yg[:, j * FC:(j + 1) * FC], OP.add)
                                return e.tensor_tensor(yg[:, j * FC:(j + 1) * FC],
                                                       yacc[:], sz_t[:], OP.mult)
                            r_g.r("vector"); r_szt.r("vector"); r_yacc.w("vector")
                            p.run("vector", vfin)
                            r_g.rdone("vector"); r_szt.rdone("vector")
                            r_yacc.wdone("vector")
                            if j == ET - 1:
                                r_yg.wdone("vector")

                if dbgstop == f"S{l}c{ch}":
                    dump_sbuf(dbg["yg"], S["yg"])
                    dump_sbuf(dbg["carry"], S["carry"])
                    p.finalize(blk, {"sync": [(k, v) for k, v in p.cnt.items() if v > 0]})
                    raise _DumpDone

                for k in range(DTL):
                    par = k % 2
                    ps, rps = psum_of(par)

                    def mo(e, k=k, ps=ps):
                        last = None
                        for q in range(2):
                            for j in range(ET):
                                last = e.matmul(
                                    ps[:, q * 512:(q + 1) * 512],
                                    W["outw"][:, j * 1024 + k * 128:j * 1024 + k * 128 + 128],
                                    yg[:, j * FC + q * 512:j * FC + (q + 1) * 512],
                                    start=(j == 0), stop=(j == ET - 1))
                        return last
                    r_yg.r("tensor"); W["r"].r("tensor"); rps.w("tensor")
                    p.run("tensor", mo)
                    r_yg.rdone("tensor"); rps.wdone("tensor")
                    tcn = p.cnt["tensor"]

                    dma("sync", xares[:],
                        xa_s.ap()[b, k * 128:(k + 1) * 128, tc0:tc0 + FC],
                        res_w=[r_xares], res_r=[r_xas], ch="lxa")

                    def vres(e, k=k, ps=ps):
                        return e.tensor_tensor(xaout[:], ps[:, 0:FC], xares[:],
                                               OP.add)
                    p.want("vector", "tensor", tcn)
                    r_xares.r("vector"); r_xaout.w("vector")
                    p.run("vector", vres)
                    rps.rdone("vector"); r_xares.rdone("vector")
                    r_xaout.wdone("vector")
                    dma("sync", xa_s.ap()[b, k * 128:(k + 1) * 128, tc0:tc0 + FC],
                        xaout[:], res_r=[r_xaout], res_w=[r_xas], ch="sxa")

        def phase_post(b, S):
            xtf, xtb, xsq = S["xtf"], S["xtb"], S["xsq"]
            r_xtf, r_xtb = S["r_xtf"], S["r_xtb"]
            accb, r_accb = S["accb"], S["r_accb"]
            ln_stats(xa_s, b, xtf, xtb, xsq, r_xtf, r_xtb, S["r_xsq"], r_src=r_xas)
            ln_rows()
            ln_replicate()
            for k in range(DTL):
                dma("sync", xtb[:], xa_s.ap()[b, k * 128:(k + 1) * 128, :],
                    res_w=[r_xtb], res_r=[r_xas], ch="lx")
                dma("sync", xtf[:], x_h.ap()[b, k * 128:(k + 1) * 128, :],
                    res_w=[r_xtf], ch="lx0")

                def ap1(e, k=k):
                    e.tensor_tensor(xsq[:], xtb[:], murep[:], OP.subtract)
                    return e.tensor_tensor(xsq[:], xsq[:], rstdrep[:], OP.mult)
                r_xtb.r("vector"); r_rep.r("vector"); S["r_xsq"].w("vector")
                p.run("vector", ap1)
                r_xtb.rdone("vector"); r_rep.rdone("vector")
                S["r_xsq"].wdone("vector")
                vc = p.cnt["vector"]

                def ap2(e, k=k):
                    e.tensor_scalar(accb[:], xsq[:],
                                    nfw_sb[:, k:k + 1], nfb_sb[:, k:k + 1],
                                    OP.mult, OP.add)
                    return e.tensor_tensor(accb[:], accb[:], xtf[:], OP.add)
                r_nf.r("vector"); r_xtf.r("vector"); r_accb.w("vector")
                p.run("vector", ap2)
                r_xtf.rdone("vector"); r_accb.wdone("vector")
                dma("sync", x2_s.ap()[b, k * 128:(k + 1) * 128, :], accb[:],
                    res_r=[r_accb], res_w=[r_x2s], ch="sx2")

        def phase_ffn_gateup(b, S, serialize=False, fence=False, sfen=False, exp=None):
            def tfence(e):
                return e.matmul(ps_dbl[0:1, 0:1], onescol_sb[:],
                                onescol_sb[:, 0:1], start=True, stop=True)

            def sfence(e):
                return e.activation(sfence_sb[0:1, 0:8], sfence_sb[0:1, 0:8],
                                    AF.Copy, bias=0.0, scale=1.0)
            """LN(x2)/127 -> xt; gate/up/hh -> hh scratch (per FF chunk)."""
            xt = S["xt"]
            x2t = S["x2t"]
            gs, us = S["gs"], S["us"]
            hh = S["hht"]
            r_xt, r_x2t = S["r_xt"], S["r_x2t"]
            r_gu, r_hh = S["r_gu"], S["r_hh"]
            for cc in range(L // FF):
                c0 = cc * FF
                for k in range(DTL):
                    dma("sync", x2t[k][:],
                        x2_s.ap()[b, k * 128:(k + 1) * 128, c0:c0 + FF],
                        res_w=[r_x2t[k]], res_r=[r_x2s], ch=f"lx2{k % 2}")

                    def apl(e, k=k, c0=c0, exp=exp):
                        if exp == "e_xtnodma":
                            e.tensor_tensor(xt[k][:], murep[:, c0:c0 + FF],
                                            murep[:, c0:c0 + FF], OP.subtract)
                            return e.tensor_tensor(xt[k][:], xt[k][:],
                                                   rstdrep[:, c0:c0 + FF],
                                                   OP.mult)
                        e.tensor_tensor(xt[k][:], x2t[k][:],
                                        murep[:, c0:c0 + FF], OP.subtract)
                        return e.tensor_tensor(xt[k][:], xt[k][:],
                                               rstdrep[:, c0:c0 + FF], OP.mult)
                    r_x2t[k].r("vector"); r_rep.r("vector"); r_xt[k].w("vector")
                    p.run("vector", apl)
                    r_x2t[k].rdone("vector"); r_xt[k].wdone("vector")
                if exp == "e_xtlag":
                    def vfence(e):
                        return e.tensor_copy(sfence_sb[0:1, 0:8],
                                             sfence_sb[0:1, 0:8])
                    p.run("vector", vfence)
                if serialize == "chunk":
                    p.barrier_into()
                for hp in range(HPT):
                    par = hp % 2
                    ps, rps = psum_of(par)

                    def mg(e, hp=hp, ps=ps, exp=exp):
                        last = None
                        for k in range(DTL):
                            mov = murep[:, 0:FF] if exp == "e_noxt" else xt[k][:]
                            last = e.matmul(
                                ps[:, 0:FF],
                                S["wg"][:, k * H3 + hp * 128:k * H3 + hp * 128 + 128],
                                mov, start=(k == 0), stop=(k == DTL - 1))
                        return last
                    for k in range(DTL):
                        r_xt[k].r("tensor")
                    S["r_w"].r("tensor"); rps.w("tensor")
                    if fence:
                        p.run("tensor", mg, inc=0)
                        p.run("tensor", tfence)
                    else:
                        p.run("tensor", mg)
                    for k in range(DTL):
                        r_xt[k].rdone("tensor")
                    rps.wdone("tensor")
                    tc1 = p.cnt["tensor"]

                    def sg(e, ps=ps):
                        return e.activation(gs[:], ps[:, 0:FF], AF.Silu,
                                            bias=0.0, scale=1.0)
                    p.want("scalar", "tensor", tc1)
                    r_gu.w("scalar")
                    if sfen:
                        p.run("scalar", sg, inc=0)
                        p.run("scalar", sfence)
                    else:
                        p.run("scalar", sg)
                    rps.rdone("scalar")

                    def mu_(e, hp=hp, ps=ps, exp=exp):
                        last = None
                        for k in range(DTL):
                            mov = murep[:, 0:FF] if exp == "e_noxt" else xt[k][:]
                            last = e.matmul(
                                ps[:, 0:FF],
                                S["wu"][:, k * H3 + hp * 128:k * H3 + hp * 128 + 128],
                                mov, start=(k == 0), stop=(k == DTL - 1))
                        return last
                    rps.w("tensor")
                    if fence:
                        p.run("tensor", mu_, inc=0)
                        p.run("tensor", tfence)
                    else:
                        p.run("tensor", mu_)
                    rps.wdone("tensor")
                    for k in range(DTL):
                        r_xt[k].rdone("tensor")
                    tc2 = p.cnt["tensor"]

                    def su(e, ps=ps):
                        return e.activation(us[:], ps[:, 0:FF], AF.Copy,
                                            bias=0.0, scale=1.0)
                    p.want("scalar", "tensor", tc2)
                    if sfen:
                        p.run("scalar", su, inc=0)
                        p.run("scalar", sfence)
                    else:
                        p.run("scalar", su)
                    rps.rdone("scalar")
                    r_gu.wdone("scalar")
                    sc1 = p.cnt["scalar"]

                    def vh(e, exp=exp):
                        if exp == "e_nosu":
                            return e.tensor_copy(hh[:], gs[:])
                        return e.scalar_tensor_tensor(hh[:], gs[:], 1.0 / 127.0,
                                                      us[:], OP.mult, OP.mult)
                    p.want("vector", "scalar", sc1)
                    r_gu.r("vector"); r_hh.w("vector")
                    p.run("vector", vh)
                    r_gu.rdone("vector"); r_hh.wdone("vector")
                    dma("sync",
                        hh_s.ap()[b, hp * 128:(hp + 1) * 128, c0:c0 + FF],
                        hh[:], res_r=[r_hh], res_w=[r_hhs], ch="shh")
                    if serialize is True or (serialize == "h4" and hp % 4 == 3):
                        p.barrier_into()

        def phase_ffn_down(b, S):
            hh = S["hhall"]
            r_hh = S["r_hhall"]
            x2t = S["x2t"]
            r_x2t = S["r_x2t"]
            outt = S["outt"]
            r_outt = S["r_outt"]
            for cc in range(L // FF):
                c0 = cc * FF
                for hp in range(HPT):
                    dma("sync", hh[hp][:],
                        hh_s.ap()[b, hp * 128:(hp + 1) * 128, c0:c0 + FF],
                        res_w=[r_hh[hp]], res_r=[r_hhs], ch=f"lhh{hp % 4}")
                for k in range(DTL):
                    par = k % 2
                    ps, rps = psum_of(par)
                    dma("sync", x2t[k][:],
                        x2_s.ap()[b, k * 128:(k + 1) * 128, c0:c0 + FF],
                        res_w=[r_x2t[k]], res_r=[r_x2s], ch="lx2a")

                    def md(e, k=k, ps=ps):
                        last = None
                        for hp in range(HPT):
                            last = e.matmul(
                                ps[:, 0:FF],
                                S["wd"][:, hp * 1024 + k * 128:hp * 1024 + k * 128 + 128],
                                hh[hp][:], start=(hp == 0), stop=(hp == HPT - 1))
                        return last
                    for hp in range(HPT):
                        r_hh[hp].r("tensor")
                    S["r_w"].r("tensor"); rps.w("tensor")
                    p.run("tensor", md)
                    for hp in range(HPT):
                        r_hh[hp].rdone("tensor")
                    rps.wdone("tensor")
                    tcn = p.cnt["tensor"]

                    def vo(e, k=k, ps=ps):
                        return e.tensor_tensor(outt[k % 2][:], ps[:, 0:FF],
                                               x2t[k][:], OP.add)
                    p.want("vector", "tensor", tcn)
                    r_x2t[k].r("vector"); r_outt[k % 2].w("vector")
                    p.run("vector", vo)
                    rps.rdone("vector"); r_x2t[k].rdone("vector")
                    r_outt[k % 2].wdone("vector")
                    dma("sync", y_h.ap()[b, k * 128:(k + 1) * 128, c0:c0 + FF],
                        outt[k % 2][:], res_r=[r_outt[k % 2]], ch=f"sy{k % 2}")

        def dump_dram(dst, srcap):
            p.barrier_into(("sync",))
            ch = p.chan("dbg")
            p.want("sync", ch, p.cnt[ch])
            p.run("sync", lambda e, d=dst, s=srcap: e.dma_start(out=d.ap(), in_=s),
                  inc=16, inc_key=ch)

        def dump_sbuf(dst, t):
            p.barrier_into(("sync",))
            ch = p.chan("dbg")
            p.want("sync", ch, p.cnt[ch])
            p.run("sync", lambda e, d=dst, t=t: e.dma_start(out=d.ap(), in_=t[:]),
                  inc=16, inc_key=ch)

        for l in range(NL):
            with ExitStack() as lctx:
                def lsb(name, shape, dtype):
                    return lctx.enter_context(
                        nc.sbuf_tensor(f"{name}_{l}", shape, dtype))
                W = {
                    "outw": lsb("outw", [128, ET * 1024], BF16),
                    "dtpw": lsb("dtpw", [RK, E], BF16),
                    "xpw": lsb("xpw", [128, ET * 96], BF16),
                    "convw": lsb("convw", [128, ET * 4], F32),
                    "convb": lsb("convb", [128, ET], F32),
                    "dtpb": lsb("dtpb", [128, ET], F32),
                    "dsk": lsb("dsk", [128, ET], F32),
                    "atmp": lsb("atmp", [128, ET * N], F32),
                    "A": lsb("A", [128, ET * N], F32),
                    "lnw": lsb("lnw", [128, DTL], F32),
                    "lnb": lsb("lnb", [128, DTL], F32),
                    "dbl_dt": [lsb(f"dbldt{b}", [RK, L], BF16) for b in range(NB)],
                    "bc": [lsb(f"bc{b}", [32, L], BF16) for b in range(NB)],
                }
                W["r"] = Res(p, f"W{l}")
                W["r_dbl"] = [Res(p, f"dbl{l}{b}") for b in range(NB)]
                p.barrier_into(("gpsimd",))
                load_small_weights(l, W)

                with ExitStack() as pctx:
                    def psb(name, shape, dtype):
                        return pctx.enter_context(
                            nc.sbuf_tensor(f"{name}_{l}", shape, dtype))
                    S = {
                        "inw": psb("inw", [128, DTL * E], BF16),
                        "lnx": psb("lnx", [128, DTL * L], BF16),
                        "xcpre": psb("xcpre", [128, L + 4], BF16),
                        "accb": psb("accb", [128, L], BF16),
                        "xcj": psb("xcj", [128, L], BF16),
                        "xtf": psb("xtf", [128, L], F32),
                        "xtb": psb("xtb", [128, L], BF16),
                        "xsq": psb("xsq", [128, L], BF16),
                    }
                    for nm in ("inw", "lnx", "xcpre", "accb", "xcj", "xtf",
                               "xtb", "xsq"):
                        S["r_" + nm] = Res(p, nm)
                    S["r_xtf"], S["r_xtb"] = S["r_xtf"], S["r_xtb"]
                    p.barrier_into()
                    def z3(e, _xp=S["xcpre"]):
                        return e.memset(_xp[:, 0:3], 0.0)
                    p.run("vector", z3)
                    S["r_xcpre"].wdone("vector")
                    for b in range(NB):
                        phase_pre(l, b, W, S)
                        if dump == f"pre{l}" and b == 0:
                            dump_dram(dbg["xa"], xa_s.ap()[0])
                            dump_sbuf(dbg["lnx"], S["lnx"])
                            p.finalize(blk, {"sync": [(k, v) for k, v in p.cnt.items() if v > 0]})
                            return
                        phase_P(l, b, W, S)
                        if dump == f"P{l}" and b == 0:
                            dump_dram(dbg["xc"], xc_s.ap()[0])
                            dump_dram(dbg["sz"], sz_s.ap()[0])
                            dump_sbuf(dbg["dbl"], W["dbl_dt"][0])
                            dump_sbuf(dbg["bc"], W["bc"][0])
                            p.finalize(blk, {"sync": [(k, v) for k, v in p.cnt.items() if v > 0]})
                            return
                    p.barrier_into()

                with ExitStack() as sctx:
                    def ssb(name, shape, dtype):
                        return sctx.enter_context(
                            nc.sbuf_tensor(f"{name}_{l}", shape, dtype))
                    S = {
                        "bcrep": ssb("bcrep", [128, 2 * NG * FC], BF16),
                        "dt_t": ssb("dt_t", [128, FC], BF16),
                        "u_t": ssb("u_t", [128, FC], BF16),
                        "xc_t": ssb("xc_t", [128, FC], BF16),
                        "sz_t": ssb("sz_t", [128, FC], BF16),
                        "dA_t": [ssb(f"dA{i}", [128, FC], F32) for i in range(2)],
                        "dBu_t": ssb("dBu_t", [128, FC], BF16),
                        "h_t": ssb("h_t", [128, FC], BF16),
                        "g_t": ssb("g_t", [128, FC], BF16),
                        "lad": [ssb(f"lad{i}", [128, FC], BF16) for i in range(3)],
                        "yacc": ssb("yacc", [128, FC], BF16),
                        "carry": ssb("carry", [128, ET * N], F32),
                        "yg": ssb("yg", [128, ET * FC], BF16),
                        "xares": ssb("xares", [128, FC], BF16),
                        "xaout": ssb("xaout", [128, FC], BF16),
                    }
                    for nm in ("bcrep", "dt", "u", "xct", "szt", "dBu", "h",
                               "g", "lad", "yacc", "carry", "yg", "xares",
                               "xaout"):
                        S["r_" + nm] = Res(p, nm)
                    S["r_dA"] = [Res(p, f"dA{i}") for i in range(2)]
                    p.barrier_into()
                    for b in range(NB):
                        try:
                            phase_S(l, b, W, S,
                                    dbgstop=dump if b == 0 else None)
                        except _DumpDone:
                            return
                        if dump == f"S{l}" and b == 0:
                            dump_dram(dbg["xa"], xa_s.ap()[0])
                            dump_sbuf(dbg["yg"], S["yg"])
                            p.finalize(blk, {"sync": [(k, v) for k, v in p.cnt.items() if v > 0]})
                            return
                    p.barrier_into()

        with ExitStack() as qctx:
            def qsb(name, shape, dtype):
                return qctx.enter_context(nc.sbuf_tensor(name, shape, dtype))
            S = {
                "xtf": qsb("q_xtf", [128, L], F32),
                "xtb": qsb("q_xtb", [128, L], BF16),
                "xsq": qsb("q_xsq", [128, L], BF16),
                "accb": qsb("q_accb", [128, L], BF16),
            }
            for nm in ("xtf", "xtb", "xsq", "accb"):
                S["r_" + nm] = Res(p, nm)
            p.barrier_into()
            for b in range(NB):
                phase_post(b, S)
                if dump == "post" and b == 0:
                    dump_dram(dbg["x2"], x2_s.ap()[0])
                    p.finalize(blk, {"sync": [(k, v) for k, v in p.cnt.items() if v > 0]})
                    return
            p.barrier_into()

        with ExitStack() as fctx:
            def fsb(name, shape, dtype):
                return fctx.enter_context(nc.sbuf_tensor(name, shape, dtype))
            S = {
                "wg": fsb("wg_sb", [128, DTL * H3], BF16),
                "wu": fsb("wu_sb", [128, DTL * H3], BF16),
                "xt": [fsb(f"f_xt{k}", [128, FF], BF16) for k in range(DTL)],
                "x2t": [fsb(f"f_x2t{k}", [128, FF], BF16) for k in range(DTL)],
                "gs": fsb("f_gs", [128, FF], BF16),
                "us": fsb("f_us", [128, FF], BF16),
                "hht": fsb("f_hht", [128, FF], BF16),
                "xtb": fsb("f_xtb", [128, L], BF16),
                "xsq": fsb("f_xsq", [128, L], BF16),
            }
            S["r_w"] = Res(p, "fw")
            S["r_xt"] = [Res(p, f"fxt{k}") for k in range(DTL)]
            S["r_x2t"] = [Res(p, f"fx2t{k}") for k in range(DTL)]
            S["r_gu"] = Res(p, "fgu")
            S["r_hh"] = Res(p, "fhh")
            S["r_xtb"] = Res(p, "fxtb")
            S["r_xtf"] = Res(p, "fxtf")
            S["r_xsq"] = Res(p, "fxsq")
            p.barrier_into()
            dma("gpsimd", S["wg"][:],
                wg_h.ap().rearrange("(k q) h -> q k h", q=128), res_w=[S["r_w"]],
                ch="w")
            dma("gpsimd", S["wu"][:],
                wu_h.ap().rearrange("(k q) h -> q k h", q=128), ch="w")
            S["r_w"].wdone("gpsimd")
            for b in range(NB):
                ln_stats(x2_s, b, None, S["xtb"], S["xsq"], None, S["r_xtb"], S["r_xsq"],
                         r_src=r_x2s)
                ln_rows(rstd_scale=1.0 / 127.0)
                ln_replicate()
                phase_ffn_gateup(b, S, serialize=({'serf': True, 'serc': 'chunk', 'serh4': 'h4'}.get(dump, False)), fence=(dump in ('v2',)), sfen=(dump in ('v3',)), exp=dump)
                if dump == "ffnA" and b == 0:
                    dump_dram(dbg["hh"], hh_s.ap()[0])
                    p.finalize(blk, {"sync": [(k, v) for k, v in p.cnt.items() if v > 0]})
                    return
            p.barrier_into()

        with ExitStack() as gctx:
            def gsb(name, shape, dtype):
                return gctx.enter_context(nc.sbuf_tensor(name, shape, dtype))
            S2 = {
                "wd": gsb("wd_sb", [128, HPT * 1024], BF16),
                "hhall": [gsb(f"g_hh{i}", [128, FF], BF16) for i in range(HPT)],
                "x2t": [gsb(f"g_x2t{k}", [128, FF], BF16) for k in range(DTL)],
                "outt": [gsb(f"g_out{i}", [128, FF], F32) for i in range(2)],
            }
            S2["r_w"] = Res(p, "gw")
            S2["r_hhall"] = [Res(p, f"ghh{i}") for i in range(HPT)]
            S2["r_x2t"] = [Res(p, f"gx2t{k}") for k in range(DTL)]
            S2["r_outt"] = [Res(p, f"gout{i}") for i in range(2)]
            p.barrier_into()
            dma("gpsimd", S2["wd"][:],
                wd_h.ap().rearrange("(j q) d -> q j d", q=128),
                res_w=[S2["r_w"]], ch="w")
            for b in range(NB):
                phase_ffn_down(b, S2)

        if dump == "full":
            for (d, s) in (("fxa", xa_s), ("fx2", x2_s), ("fhh", hh_s)):
                p.barrier_into(("sync",))
                ch = p.chan("dbg")
                p.want("sync", ch, p.cnt[ch])
                p.run("sync", lambda e, d=dbg[d], s=s: e.dma_start(out=d.ap(),
                                                                  in_=s.ap()),
                      inc=16, inc_key=ch)

        final = {"sync": [(k, v) for k, v in p.cnt.items() if v > 0]}
        p.finalize(blk, final)
    return nc


_SEL = None


def _consts():
    global _SEL
    if _SEL is None:
        sel = np.zeros((32, 32, 128), np.float32)
        for r in range(32):
            sel[r, r, :] = 1.0
        _SEL = sel.reshape(32, 32 * 128).astype(BB)
    return _SEL


def kernel(L0_attn_x, L0_attn_ln_w, L0_attn_ln_b, L0_attn_in_proj_w,
           L0_attn_conv_w, L0_attn_conv_b, L0_attn_x_proj_w, L0_attn_dt_proj_w,
           L0_attn_dt_proj_b, L0_attn_A_log, L0_attn_D_skip, L0_attn_out_proj_w,
           L0_attn_norm_f_w, L0_attn_norm_f_b, L0_ffn_w_gate, L0_ffn_w_up,
           L0_ffn_w_down):
    x = np.asarray(L0_attn_x, np.float32)
    xT = np.ascontiguousarray(x.transpose(0, 2, 1))
    inw = np.ascontiguousarray(np.asarray(L0_attn_in_proj_w, np.float32)
                               .transpose(0, 2, 1))
    xpw = np.ascontiguousarray(np.asarray(L0_attn_x_proj_w, np.float32)
                               .transpose(0, 2, 1))
    dtpw = np.ascontiguousarray(np.asarray(L0_attn_dt_proj_w, np.float32)
                                .transpose(0, 2, 1))
    outw = np.ascontiguousarray(np.asarray(L0_attn_out_proj_w, np.float32)
                                .transpose(0, 2, 1))
    wg = np.ascontiguousarray(np.asarray(L0_ffn_w_gate, np.int32).T)
    wu = np.ascontiguousarray(np.asarray(L0_ffn_w_up, np.int32).T)
    wd = np.ascontiguousarray(np.asarray(L0_ffn_w_down, np.int32).T)

    sel = _consts()
    ones1 = np.ones((1, 128), np.float32).astype(BB)
    ones1f = np.ones((1, 128), np.float32)
    onescol = np.ones((128, 1), np.float32).astype(BB)

    def pp2(a):
        a = np.asarray(a, np.float32)
        t = a.shape[1] // 128
        return np.ascontiguousarray(a.reshape(NL, t, 128).transpose(0, 2, 1))

    def pp3(a):
        a = np.asarray(a, np.float32)
        t, k = a.shape[1] // 128, a.shape[2]
        return np.ascontiguousarray(a.reshape(NL, t, 128, k).transpose(0, 2, 1, 3))

    def pp1(a):
        a = np.asarray(a, np.float32)
        return np.ascontiguousarray(a.reshape(-1, 128).T)

    common = dict(
        lnw=pp2(L0_attn_ln_w),
        lnb=pp2(L0_attn_ln_b),
        inw=inw,
        convw=pp3(L0_attn_conv_w),
        convb=pp2(L0_attn_conv_b),
        xpw=xpw,
        dtpw=dtpw,
        dtpb=pp2(L0_attn_dt_proj_b),
        alog=pp3(L0_attn_A_log),
        dsk=pp2(L0_attn_D_skip),
        outw=outw,
        nfw=pp1(L0_attn_norm_f_w),
        nfb=pp1(L0_attn_norm_f_b),
        wg=wg, wu=wu, wd=wd,
        sel=sel, ones1=ones1, ones1f=ones1f, onescol=onescol,
    )
    in_maps = []
    for c in range(8):
        m = dict(common)
        m["x"] = np.ascontiguousarray(xT[2 * c:2 * c + 2])
        in_maps.append(m)

    nc = bass.Bass("TRN2", target_bir_lowering=False)
    build(nc)
    res = run_bass_kernel_spmd(nc, in_maps, core_ids=list(range(8)))
    outs = []
    for c in range(8):
        yc = np.asarray(res.results[c]["y"], np.float32)
        outs.append(yc.transpose(0, 2, 1))
    return np.ascontiguousarray(np.concatenate(outs, axis=0))
